# version 01
# baseline (speedup 1.0000x reference)
import contextlib
import numpy as np
import concourse.bass as bass
import concourse.mybir as mybir
from concourse.bass_utils import run_bass_kernel_spmd

F32 = mybir.dt.float32
BF16 = mybir.dt.bfloat16
F32R = mybir.dt.float32r
AF = mybir.ActivationFunctionType
ALU = mybir.AluOpType
AX = mybir.AxisListType.X

D = 1024
DC = 8
H = 16
DH = 64
NCORES = 8


class Ctx:

    def __init__(self, active, eng, sems):
        self.active = active
        self.eng = eng
        self.h = sems
        self.cnt = {k: 0 for k in sems}

    def _fin(self, instr, inc, n=1):
        if inc is not None:
            self.cnt[inc] += n
            if instr is not None:
                instr.then_inc(self.h[inc], n)
        return instr

    def w(self, eng, sem, val):
        if self.active == eng and val > 0:
            self.eng.wait_ge(self.h[sem], val)

    def now(self, sem):
        return self.cnt[sem]

    def drain(self, eng):
        if self.active == eng:
            self.eng.drain()

    def pe(self, f):
        return self._fin(f(self.eng) if self.active == "pe" else None, "pe")

    def dve(self, f):
        return self._fin(f(self.eng) if self.active == "dve" else None, "dve")

    def act(self, f):
        return self._fin(f(self.eng) if self.active == "act" else None, "act")

    def gp(self, f):
        return self._fin(f(self.eng) if self.active == "gp" else None, "gp")

    def dma_sp(self, out, in_, sem):
        i = self.eng.dma_start(out=out, in_=in_) if self.active == "sp" else None
        return self._fin(i, sem, 16)

    def dma_gp(self, out, in_, sem):
        i = self.eng.dma_start(out=out, in_=in_) if self.active == "gp" else None
        return self._fin(i, sem, 16)


def build_nc(S=2048, upto=9):
    KC = S // 128
    QC = S // 512
    TT = S // 128
    nc = bass.Bass("TRN2", target_bir_lowering=False)

    dx = nc.dram_tensor("x", [S, D], F32, kind="ExternalInput")
    dwqkv = nc.dram_tensor("wqkvT", [D, 3 * D], F32, kind="ExternalInput")
    dwo = nc.dram_tensor("woT", [D, D], F32, kind="ExternalInput")
    dwr = nc.dram_tensor("wrT", [D, D], F32, kind="ExternalInput")
    dwk = nc.dram_tensor("wkT", [D, D], F32, kind="ExternalInput")
    dwv = nc.dram_tensor("wvT", [D, D], F32, kind="ExternalInput")
    dbqkv = nc.dram_tensor("bqkv", [128, 24], F32, kind="ExternalInput")
    dbo = nc.dram_tensor("bo", [128, 8], F32, kind="ExternalInput")
    dmur = nc.dram_tensor("mur", [128, 8], F32, kind="ExternalInput")
    dmuk = nc.dram_tensor("muk", [128, 8], F32, kind="ExternalInput")
    dwq2 = nc.dram_tensor("wq2", [128, 1], F32, kind="ExternalInput")
    dwk2 = nc.dram_tensor("wk2", [128, 1], F32, kind="ExternalInput")
    dy = nc.dram_tensor("y", [S, D], F32, kind="ExternalOutput")

    BASE = (nc.sbuf_base + 31) // 32 * 32
    off = [BASE]

    def alloc(name, shape, dtype, at=None):
        per = int(np.prod(shape[1:])) * mybir.dt.size(dtype)
        o = off[0] if at is None else at
        t = nc.alloc_sbuf_tensor_at(name, shape, dtype, offset=o)
        if at is None:
            off[0] = o + (per + 31) // 32 * 32
        return t

    ES = S
    x_sb = alloc("x_sb", [128, DC * S], F32)
    o_qt = off[0]
    qt = alloc("qt", [128, DC * S], BF16)
    xk = alloc("xk", [128, DC * S], BF16, at=o_qt)
    o_kt = off[0]
    kt = alloc("kt", [128, DC * S], BF16)
    rsig = alloc("rsig", [128, DC * S], BF16, at=o_kt)
    o_vt = off[0]
    vt = alloc("vt", [128, KC * H * (DH + 1)], BF16)
    xhat2 = alloc("xhat2", [128, DC * S], BF16, at=o_vt)
    kact = alloc("kact", [128, DC * S], BF16, at=o_vt)
    o_E = off[0]
    E_BYTES = max(2 * DC * S, 4096 + DC * 512 * 2 + 4 * DC * 128 * 2, 2 * D * 4)
    xhat = alloc("xhat", [128, DC * S], BF16, at=o_E)
    off[0] = o_E + E_BYTES
    instage = alloc("instage", [128, 2 * D], F32, at=o_E)
    outstage = alloc("outstage", [128, 2 * D], F32, at=o_E)
    xr = alloc("xr", [128, DC * S], BF16, at=o_E)
    attn = alloc("attn", [128, 2 * 2 * 512], BF16, at=o_E)
    zs = alloc("zs", [128, DC * 512], BF16, at=o_E + 4096)
    wos = alloc("wos", [128, 4 * DC * 128], BF16, at=o_E + 4096 + 8192)
    wst = alloc("wst", [128, 2 * DC * 128], BF16)
    tq = alloc("tq", [128, 512], F32)
    sqb = alloc("sqb", [128, 512], BF16)
    xbt = alloc("xbt", [128, 512], BF16)
    xqt = alloc("xqt", [128, 512], BF16)
    stats = alloc("stats", [128, 512], F32)
    statr = alloc("statr", [128, 512], F32R)
    small = alloc("small", [128, 32], F32)
    aones_b = alloc("aones_b", [128, 1], BF16)
    honesA_b = alloc("honesA_b", [128, 1], BF16)
    honesB_b = alloc("honesB_b", [128, 1], BF16)
    bqkv_sb = alloc("bqkv_sb", [128, 24], F32)
    bo_sb = alloc("bo_sb", [128, 8], F32)
    mur_sb = alloc("mur_sb", [128, 8], F32)
    muk_sb = alloc("muk_sb", [128, 8], F32)
    wq2_sb = alloc("wq2_sb", [128, 1], F32)
    wk2_sb = alloc("wk2_sb", [128, 1], F32)
    identf = alloc("identf", [128, 128], F32)
    rowsr = alloc("rowsr", [128, 256], F32R)
    nb = alloc("nb", [128, 8], F32)
    nbr = alloc("nbr", [128, 8], F32R)
    norm_acc = alloc("norm_acc", [128, 72], F32)
    csts = alloc("csts", [128, 6], F32)
    assert off[0] <= 229344, f"sbuf overflow: {off[0]}"
    nc.alloc_sbuf_tensor("reserve_manual", [128, off[0] - BASE], mybir.dt.uint8)
    rowsf = tq
    lrec = stats
    lrec_r = statr
    rz_sb = tq
    sqf = tq

    pA = nc.alloc_psum_tensor("pA", [128, 1024], F32)
    pB = nc.alloc_psum_tensor("pB", [128, 1024], F32)
    pZ = nc.alloc_psum_tensor("pZ", [128, 1024], F32)
    pM = nc.alloc_psum_tensor("pM", [128, 512], F32)
    pT = nc.alloc_psum_tensor("pT", [128, 512], F32)

    nc.const_aps.register(F32, 1e-5, csts[:, 0:1])
    nc.const_aps.register(F32, 1e-6, csts[:, 1:2])
    nc.const_aps.register(F32, 1.0 / DH, csts[:, 2:3])

    sem_names = ["pe", "dve", "act", "gp", "ldp", "lq0", "lq1",
                 "wo0", "wo1", "wo2", "wo3", "ldh0", "ldh1", "ou0", "ou1"]
    cm = contextlib.ExitStack()
    sems = {n: cm.enter_context(nc.semaphore("s_" + n)) for n in sem_names}

    def xs(dc, a, b):
        return x_sb[:, dc * ES + a: dc * ES + b]

    def program(c: Ctx):
        c.gp(lambda e: e.iota(identf[:, :], [[-1, 128]], channel_multiplier=1,
                              allow_small_or_imprecise_dtypes=True))
        c.gp(lambda e: e.memset(rowsf[:, :], 0.0))
        c.drain("gp")
        for _r in (0, 32, 64):
            c.gp(lambda e, _r=_r: e.memset(rowsf[_r:_r + 1, 0:128], 1.0))
            c.drain("gp")
        for _r in (0, 64):
            c.gp(lambda e, _r=_r: e.memset(rowsf[_r:_r + 1, 128:192], 1.0))
            c.drain("gp")
        c.gp(lambda e: e.memset(rowsf[32:33, 192:256], 1.0))
        c.drain("gp")
        c.gp(lambda e: e.memset(aones_b[:, :], 1.0))
        c.drain("gp")
        c.gp(lambda e: e.memset(honesA_b[:, :], 0.0))
        c.drain("gp")
        c.gp(lambda e: e.memset(honesA_b[0:64, :], 1.0))
        c.drain("gp")
        c.gp(lambda e: e.memset(honesB_b[:, :], 0.0))
        c.drain("gp")
        c.gp(lambda e: e.memset(honesB_b[64:128, :], 1.0))
        c.drain("gp")
        c.gp(lambda e: e.memset(vt[:, :], 1.0))
        c.drain("gp")
        c.gp(lambda e: e.memset(csts[:, 0:1], 1e-5))
        c.drain("gp")
        c.gp(lambda e: e.memset(csts[:, 1:2], 1e-6))
        c.drain("gp")
        c.gp(lambda e: e.memset(csts[:, 2:3], 1.0 / DH))
        c.drain("gp")
        c.gp(lambda e: e.memset(csts[:, 3:4], 0.0))
        c.drain("gp")
        c.gp(lambda e: e.memset(csts[:, 4:5], 1.0))
        c.drain("gp")
        gp_const = c.now("gp")

        c.dma_gp(bqkv_sb[:, :], dbqkv.ap(), "ldp")
        c.dma_gp(bo_sb[:, :], dbo.ap(), "ldp")
        c.dma_gp(mur_sb[:, :], dmur.ap(), "ldp")
        c.dma_gp(muk_sb[:, :], dmuk.ap(), "ldp")
        c.dma_gp(wq2_sb[:, :], dwq2.ap(), "ldp")
        c.dma_gp(wk2_sb[:, :], dwk2.ap(), "ldp")
        lds_params = c.now("ldp")

        c.w("dve", "gp", gp_const)
        c.dve(lambda e: e.tensor_single_scalar(identf[:, :], identf[:, :], 0.0, ALU.is_equal))
        c.dve(lambda e: e.tensor_copy(rowsr[:, :], rowsf[:, 0:256]))
        c.w("dve", "ldp", lds_params)
        c.dve(lambda e: e.tensor_mul(small[:, 0:1], wq2_sb[:, :], wk2_sb[:, :]))
        dve_const = c.now("dve")

        dve_tile_mark = []
        for t in range(TT):
            if t >= 2:
                c.w("sp", "dve", dve_tile_mark[t - 2])
            hsem = "ldh" + str(t % 2)
            c.dma_sp(instage[:, (t % 2) * D:(t % 2) * D + D],
                     dx.ap()[t * 128:(t + 1) * 128, :], hsem)
            c.w("pe", hsem, c.now(hsem))
            if t == 0:
                c.w("pe", "dve", dve_const)
            for g in range(2):
                if t > 0 or g > 0:
                    c.w("pe", "dve", c.now("dve"))
                for j in range(4):
                    dcj = g * 4 + j
                    c.pe(lambda e, t=t, dcj=dcj, j=j: e.transpose(
                        pT[:, j * 128: j * 128 + 128],
                        instage[:, (t % 2) * D + dcj * 128:(t % 2) * D + dcj * 128 + 128],
                        identf[:, :]))
                pe_here = c.now("pe")
                c.w("dve", "pe", pe_here)
                for j in range(4):
                    dcj = g * 4 + j
                    c.dve(lambda e, t=t, dcj=dcj, j=j: e.tensor_copy(
                        xs(dcj, t * 128, t * 128 + 128),
                        pT[:, j * 128: j * 128 + 128]))
            dve_tile_mark.append(c.now("dve"))
        x_ready = c.now("dve")

        if upto < 1:
            return finish(c)
        def layernorm(dst, qc):
            q0 = qc * 512
            for dc in range(DC):
                c.w("dve", "pe", c.now("pe"))
                c.dve(lambda e, dc=dc, q0=q0: e.tensor_copy(
                    xbt[:, :], xs(dc, q0, q0 + 512)))
                c.dve(lambda e, dc=dc, q0=q0: e.tensor_mul(
                    xqt[:, :], xs(dc, q0, q0 + 512), xs(dc, q0, q0 + 512)))
                d_here = c.now("dve")
                c.w("pe", "dve", d_here)
                c.pe(lambda e, dc=dc: e.matmul(
                    pB[0:1, 0:512], aones_b[:, :], xbt[:, :],
                    start=(dc == 0), stop=(dc == DC - 1), tile_position=(0, 0)))
                c.pe(lambda e, dc=dc: e.matmul(
                    pB[32:33, 512:1024], aones_b[:, :], xqt[:, :],
                    start=(dc == 0), stop=(dc == DC - 1), tile_position=(0, 32)))
            pe_stats = c.now("pe")
            c.w("dve", "pe", pe_stats)
            c.dve(lambda e: e.tensor_copy(statr[0:1, :], pB[0:1, 0:512]))
            c.dve(lambda e: e.tensor_copy(statr[32:33, :], pB[32:33, 512:1024]))
            dve_sr = c.now("dve")
            c.w("pe", "dve", dve_sr)
            c.pe(lambda e: e.matmul(pM[:, 0:512], rowsr[0:1, 0:128], statr[0:1, :],
                                    start=True, stop=True))
            c.pe(lambda e: e.matmul(pB[:, 0:512], rowsr[32:33, 0:128], statr[32:33, :],
                                    start=True, stop=True))
            pe_bc = c.now("pe")
            c.w("dve", "pe", pe_bc)
            c.drain("dve")
            c.dve(lambda e: e.tensor_scalar(tq[:, 0:512], pM[:, 0:512],
                                            1.0 / D, None, ALU.mult))
            c.drain("dve")
            c.dve(lambda e: e.tensor_mul(stats[:, :], tq[:, 0:512], tq[:, 0:512]))
            c.drain("dve")
            c.dve(lambda e: e.scalar_tensor_tensor(stats[:, :], pB[:, 0:512],
                                                   1.0 / D, stats[:, :],
                                                   ALU.mult, ALU.subtract))
            dve_v = c.now("dve")
            c.w("act", "dve", dve_v)
            c.act(lambda e: e.activation(pM[:, 0:512], stats[:, :], AF.Sqrt,
                                         bias=1e-5, scale=1.0))
            act_sd = c.now("act")
            c.w("dve", "act", act_sd)
            c.dve(lambda e: e.reciprocal(stats[:, :], pM[:, 0:512]))
            c.drain("dve")
            c.dve(lambda e: e.scalar_tensor_tensor(pM[:, 0:512], tq[:, 0:512], -1.0,
                                                   stats[:, :], ALU.mult, ALU.mult))
            c.drain("dve")
            dve_ab = c.now("dve")
            for dc in range(DC):
                c.dve(lambda e, dc=dc, q0=q0: e.tensor_mul(
                    tq[:, 0:512], xs(dc, q0, q0 + 512), stats[:, :]))
                c.drain("dve")
                c.dve(lambda e, dc=dc, q0=q0: e.tensor_add(
                    dst[:, dc * ES + q0: dc * ES + q0 + 512], tq[:, 0:512],
                    pM[:, 0:512]))
                c.drain("dve")
            return c.now("pe"), c.now("dve")

        c.w("pe", "dve", x_ready)
        for qc in range(QC):
            layernorm(xhat, qc)
        ln1_pe, ln1_dve = c.now("pe"), c.now("dve")

        if upto < 2:
            return finish(c)
        def wqkv_ap(oc):
            return bass.AP(dwqkv, oc * 128, [[3 * D, 128], [128 * 3 * D, DC], [1, 128]])

        qkv_pe_mark = [0] * 24
        lds_mark = [0] * 24
        evac_mark = [(0, 0, 0)] * (24 * QC)
        chain = 0
        for oc in range(24):
            if oc == 0:
                c.dma_gp(wst[:, 0:DC * 128], wqkv_ap(0), "lq0")
                lds_mark[0] = c.now("lq0")
                c.dma_gp(wst[:, DC * 128:2 * DC * 128], wqkv_ap(1), "lq1")
                lds_mark[1] = c.now("lq1")
            elif oc + 1 < 24:
                c.w("gp", "pe", qkv_pe_mark[oc - 1])
                sl = ((oc + 1) % 2) * DC * 128
                c.dma_gp(wst[:, sl:sl + DC * 128], wqkv_ap(oc + 1), "lq" + str((oc + 1) % 2))
                lds_mark[oc + 1] = c.now("lq" + str((oc + 1) % 2))
            c.w("pe", "lq" + str(oc % 2), lds_mark[oc])
            if oc == 0:
                c.w("pe", "dve", ln1_dve)
            wsl = (oc % 2) * DC * 128
            for qc in range(QC):
                q0 = qc * 512
                bank = (chain % 2) * 512
                if chain >= 2:
                    am, dm, _pm = evac_mark[chain - 2]
                    c.w("pe", "act", am)
                    c.w("pe", "dve", dm)
                for dc in range(DC):
                    c.pe(lambda e, dc=dc, q0=q0, bank=bank, wsl=wsl: e.matmul(
                        pA[:, bank:bank + 512],
                        wst[:, wsl + dc * 128: wsl + dc * 128 + 128],
                        xhat[:, dc * ES + q0: dc * ES + q0 + 512],
                        start=(dc == 0), stop=(dc == DC - 1)))
                pe_mm = c.now("pe")
                tsl = 0
                c.w("act", "pe", pe_mm)
                if chain >= 1:
                    c.w("act", "dve", evac_mark[chain - 1][1])
                    c.w("act", "pe", evac_mark[chain - 1][2])
                c.act(lambda e, oc=oc, bank=bank, tsl=tsl: e.activation(
                    tq[:, tsl:tsl + 512], pA[:, bank:bank + 512], AF.Identity,
                    bias=bqkv_sb[:, oc:oc + 1], scale=1.0))
                act_t = c.now("act")
                if oc < 16:
                    pair = oc % 8
                    c.w("dve", "act", act_t)
                    c.dve(lambda e, tsl=tsl: e.tensor_mul(
                        sqb[:, :], tq[:, tsl:tsl + 512], tq[:, tsl:tsl + 512]))
                    dve_sq = c.now("dve")
                    c.w("pe", "dve", dve_sq)
                    c.pe(lambda e: e.matmul(pB[0:1, 0:512], honesA_b[:, :], sqb[:, :],
                                            start=True, stop=True, tile_position=(0, 0)))
                    c.pe(lambda e: e.matmul(pB[32:33, 512:1024], honesB_b[:, :], sqb[:, :],
                                            start=True, stop=True, tile_position=(0, 32)))
                    pe_rms = c.now("pe")
                    c.w("dve", "pe", pe_rms)
                    c.drain("dve")
                    c.dve(lambda e: e.tensor_copy(statr[0:1, :], pB[0:1, 0:512]))
                    c.dve(lambda e: e.tensor_copy(statr[32:33, :], pB[32:33, 512:1024]))
                    dve_sr2 = c.now("dve")
                    c.w("pe", "dve", dve_sr2)
                    c.pe(lambda e: e.matmul(pB[:, 0:512], rowsr[0:1, 0:128], statr[0:1, :],
                                            start=True, stop=True))
                    c.pe(lambda e: e.matmul(pB[:, 512:1024], rowsr[32:33, 0:128], statr[32:33, :],
                                            start=True, stop=True))
                    pe_ssq = c.now("pe")
                    c.w("act", "pe", pe_ssq)
                    c.act(lambda e: e.activation(pM[0:64, 0:512], pB[0:64, 0:512], AF.Sqrt,
                                                 bias=1e-6, scale=1.0 / DH))
                    c.act(lambda e: e.activation(pM[64:128, 0:512], pB[64:128, 512:1024], AF.Sqrt,
                                                 bias=csts[64:128, 1:2], scale=csts[64:128, 2:3]))
                    act_sd2 = c.now("act")
                    c.w("dve", "act", act_sd2)
                    c.drain("dve")
                    c.dve(lambda e: e.reciprocal(stats[:, :], pM[:, 0:512]))
                    c.drain("dve")
                    dve_rsq = c.now("dve")
                    if oc < 8:
                        c.dve(lambda e, pair=pair, q0=q0, tsl=tsl: e.scalar_tensor_tensor(
                            qt[:, pair * ES + q0: pair * ES + q0 + 512],
                            tq[:, tsl:tsl + 512], small[:, 0:1], stats[:, :],
                            ALU.mult, ALU.mult))
                    else:
                        c.dve(lambda e, pair=pair, q0=q0, tsl=tsl: e.tensor_mul(
                            kt[:, pair * ES + q0: pair * ES + q0 + 512],
                            tq[:, tsl:tsl + 512], stats[:, :]))
                    c.drain("dve")
                    dstt = qt if oc < 8 else kt
                    c.dve(lambda e, pair=pair, q0=q0, dstt=dstt: e.tensor_mul(
                        sqb[:, :], dstt[:, pair * ES + q0: pair * ES + q0 + 512],
                        dstt[:, pair * ES + q0: pair * ES + q0 + 512]))
                    dve_csq = c.now("dve")
                    c.w("pe", "dve", dve_csq)
                    c.pe(lambda e: e.matmul(pM[0:1, 0:512], honesA_b[:, :], sqb[:, :],
                                            start=True, stop=True, tile_position=(0, 0)))
                    c.pe(lambda e: e.matmul(pM[32:33, 0:512], honesB_b[:, :], sqb[:, :],
                                            start=True, stop=True, tile_position=(0, 32)))
                    pe_cs = c.now("pe")
                    col = (0 if oc < 8 else 36) + pair * QC + qc
                    c.w("dve", "pe", pe_cs)
                    c.dve(lambda e, col=col: e.reduce_max(
                        norm_acc[0:1, col:col + 1], pM[0:1, 0:512], axis=AX))
                    c.dve(lambda e, col=col: e.reduce_max(
                        norm_acc[32:33, col:col + 1], pM[32:33, 0:512], axis=AX))
                else:
                    vc = oc - 16
                    c.w("pe", "act", act_t)
                    if chain >= 1:
                        c.w("pe", "dve", c.now("dve"))
                    for tb in range(4):
                        c.pe(lambda e, tb=tb, tsl=tsl: e.transpose(
                            pT[:, tb * 128:tb * 128 + 128],
                            tq[:, tsl + tb * 128: tsl + tb * 128 + 128], identf[:, :]))
                    pe_vt = c.now("pe")
                    c.w("dve", "pe", pe_vt)
                    for tb in range(4):
                        kc = qc * 4 + tb
                        base = kc * H * (DH + 1) + vc * 2 * (DH + 1)
                        c.dve(lambda e, tb=tb, base=base: e.tensor_copy(
                            bass.AP(vt, base, [[KC * H * (DH + 1), 128], [DH + 1, 2], [1, DH]]),
                            pT[:, tb * 128:tb * 128 + 128].rearrange("p (g n) -> p g n", g=2)))
                evac_mark[chain] = (c.now("act"), c.now("dve"), c.now("pe"))
                chain += 1
            qkv_pe_mark[oc] = c.now("pe")
        qkv_pe = c.now("pe")

        if upto < 3:
            return finish(c)
        c.w("dve", "pe", qkv_pe)
        c.drain("dve")
        for pair in range(8):
            qa, ka = pair * QC, 36 + pair * QC
            for r in (0, 32):
                c.dve(lambda e, qa=qa, r=r: e.reduce_max(
                    small[r:r + 1, 2:3], norm_acc[r:r + 1, qa:qa + QC], axis=AX))
                c.dve(lambda e, ka=ka, r=r: e.reduce_max(
                    small[r:r + 1, 3:4], norm_acc[r:r + 1, ka:ka + QC], axis=AX))
                c.drain("dve")
                c.dve(lambda e, r=r: e.tensor_mul(small[r:r + 1, 4:5],
                                                  small[r:r + 1, 2:3], small[r:r + 1, 3:4]))
                c.drain("dve")
            dve_cs2 = c.now("dve")
            c.w("act", "dve", dve_cs2)
            c.act(lambda e: e.activation(small[0:1, 5:6], small[0:1, 4:5], AF.Sqrt))
            c.act(lambda e: e.activation(small[32:33, 5:6], small[32:33, 4:5], AF.Sqrt,
                                         bias=csts[32:33, 3:4], scale=csts[32:33, 4:5]))
            act_cs = c.now("act")
            c.w("dve", "act", act_cs)
            for r in (0, 32):
                c.dve(lambda e, pair=pair, r=r: e.tensor_scalar(
                    nb[r:r + 1, pair:pair + 1], small[r:r + 1, 5:6],
                    -1.0, 71.0, ALU.mult, ALU.add))
                c.drain("dve")
                c.dve(lambda e, pair=pair, r=r: e.tensor_copy(
                    nbr[r:r + 1, pair:pair + 1], nb[r:r + 1, pair:pair + 1]))
                c.drain("dve")
        dve_nb = c.now("dve")
        c.w("pe", "dve", dve_nb)
        c.pe(lambda e: e.matmul(pM[:, 0:8], rowsr[0:1, 0:128],
                                nbr[0:1, 0:8], start=True, stop=True))
        c.pe(lambda e: e.matmul(pM[:, 8:16], rowsr[32:33, 0:128],
                                nbr[32:33, 0:8], start=True, stop=True))
        pe_nb = c.now("pe")
        c.w("dve", "pe", pe_nb)
        for pair in range(8):
            for hh in range(2):
                c.dve(lambda e, pair=pair, hh=hh: e.tensor_copy(
                    small[:, 16 + 2 * pair + hh:17 + 2 * pair + hh],
                    pM[:, hh * 8 + pair:hh * 8 + pair + 1]))
        dve_bias = c.now("dve")

        if upto < 4:
            return finish(c)
        last_exp = {0: 0, 1: 0}
        last_z = {0: 0, 1: 0}
        wo_dve_mark = 0
        wo_pe_mark = [0] * 4
        lds_wo = [0] * (QC * 8)
        first_att = True

        zevac_dve = 0
        for qc in range(QC):
            q0 = qc * 512
            for pair in range(8):
                if first_att:
                    c.w("pe", "dve", dve_bias)
                    first_att = False
                if pair == 0 and qc > 0:
                    c.w("pe", "dve", wo_dve_mark)
                if pair < 4:
                    toc = pair
                    slot = toc % 4
                    if qc == 0:
                        c.w("gp", "pe", qkv_pe)
                    c.w("gp", "pe", wo_pe_mark[slot])
                    c.dma_gp(wos[:, slot * DC * 128:(slot + 1) * DC * 128],
                             bass.AP(dwo, toc * 128, [[D, 128], [128 * D, DC], [1, 128]]),
                             "wo" + str(slot))
                    lds_wo[qc * 8 + toc] = c.now("wo" + str(slot))

                zm = {}
                for kc in range(KC):
                    sl = kc % 2
                    c.w("pe", "act", last_exp[sl])
                    c.pe(lambda e, pair=pair, kc=kc, sl=sl, q0=q0: e.matmul(
                        pA[:, sl * 512: sl * 512 + 512],
                        kt[0:64, pair * ES + kc * 128: pair * ES + kc * 128 + 128],
                        qt[0:64, pair * ES + q0: pair * ES + q0 + 512],
                        start=True, stop=True, tile_position=(0, 0)))
                    c.pe(lambda e, pair=pair, kc=kc, sl=sl, q0=q0: e.matmul(
                        pB[:, sl * 512: sl * 512 + 512],
                        kt[64:128, pair * ES + kc * 128: pair * ES + kc * 128 + 128],
                        qt[64:128, pair * ES + q0: pair * ES + q0 + 512],
                        start=True, stop=True, tile_position=(64, 0)))
                    sc_pe = c.now("pe")
                    c.w("act", "pe", max(sc_pe, last_z[sl]))
                    c.act(lambda e, pair=pair, sl=sl: e.activation(
                        attn[:, (0 * 2 + sl) * 512:(0 * 2 + sl) * 512 + 512],
                        pA[:, sl * 512: sl * 512 + 512], AF.Exp,
                        bias=small[:, 16 + 2 * pair:17 + 2 * pair], scale=1.0))
                    c.act(lambda e, pair=pair, sl=sl: e.activation(
                        attn[:, (1 * 2 + sl) * 512:(1 * 2 + sl) * 512 + 512],
                        pB[:, sl * 512: sl * 512 + 512], AF.Exp,
                        bias=small[:, 17 + 2 * pair:18 + 2 * pair], scale=1.0))
                    last_exp[sl] = c.now("act")
                    c.w("pe", "act", last_exp[sl])
                    if kc == 0:
                        c.w("pe", "dve", zevac_dve)
                    for hh in range(2):
                        hl = 2 * pair + hh
                        c.pe(lambda e, hh=hh, hl=hl, kc=kc, sl=sl: e.matmul(
                            pZ[0:65, hh * 512: hh * 512 + 512],
                            vt[:, kc * H * (DH + 1) + hl * (DH + 1):
                               kc * H * (DH + 1) + hl * (DH + 1) + DH + 1],
                            attn[:, (hh * 2 + sl) * 512:(hh * 2 + sl) * 512 + 512],
                            start=(kc == 0), stop=(kc == KC - 1)))
                    last_z[sl] = c.now("pe")
                z_pe = c.now("pe")
                for hh in range(2):
                    c.w("dve", "pe", z_pe)
                    c.dve(lambda e, hh=hh: e.reciprocal(lrec[0:1, 0:512],
                                                        pZ[64:65, hh * 512: hh * 512 + 512]))
                    c.drain("dve")
                    c.dve(lambda e: e.tensor_copy(lrec_r[0:1, 0:512], lrec[0:1, 0:512]))
                    dve_lr = c.now("dve")
                    c.w("pe", "dve", dve_lr)
                    c.pe(lambda e: e.matmul(pM[:, 0:512], rowsr[0:1, 128:256], lrec_r[0:1, 0:512],
                                            start=True, stop=True))
                    pe_rz = c.now("pe")
                    c.w("dve", "pe", pe_rz)
                    c.dve(lambda e: e.tensor_copy(rz_sb[0:64, 0:512], pM[0:64, 0:512]))
                    c.drain("dve")
                    r0 = hh * 64
                    c.dve(lambda e, hh=hh, pair=pair, r0=r0: e.tensor_mul(
                        zs[r0:r0 + 64, pair * 512:(pair + 1) * 512],
                        pZ[0:64, hh * 512: hh * 512 + 512], rz_sb[0:64, 0:512]))
                zevac_dve = c.now("dve")
            zs_dve = c.now("dve")
            att_act = c.now("act")
            wo_evac = 0
            for oc in range(8):
                bank = (oc % 2) * 512
                c.w("pe", "wo" + str(oc % 4), lds_wo[qc * 8 + oc])
                if oc == 0:
                    c.w("pe", "dve", zs_dve)
                    c.w("pe", "act", att_act)
                if oc >= 2:
                    c.w("pe", "dve", c.now("dve"))
                slot = oc % 4
                for dc in range(DC):
                    c.pe(lambda e, oc=oc, dc=dc, bank=bank, slot=slot: e.matmul(
                        pA[:, bank:bank + 512],
                        wos[:, slot * DC * 128 + dc * 128: slot * DC * 128 + dc * 128 + 128],
                        zs[:, dc * 512:(dc + 1) * 512],
                        start=(dc == 0), stop=(dc == DC - 1)))
                wo_pe_mark[slot] = c.now("pe")
                pe_wo = c.now("pe")
                if oc + 4 < 8:
                    toc = oc + 4
                    c.w("gp", "pe", wo_pe_mark[slot])
                    c.dma_gp(wos[:, slot * DC * 128:(slot + 1) * DC * 128],
                             bass.AP(dwo, toc * 128, [[D, 128], [128 * D, DC], [1, 128]]),
                             "wo" + str(slot))
                    lds_wo[qc * 8 + toc] = c.now("wo" + str(slot))
                c.w("dve", "pe", pe_wo)
                c.dve(lambda e, oc=oc, q0=q0, bank=bank: e.scalar_tensor_tensor(
                    xs(oc, q0, q0 + 512), pA[:, bank:bank + 512], bo_sb[:, oc:oc + 1],
                    xs(oc, q0, q0 + 512), ALU.add, ALU.add))
            wo_dve_mark = c.now("dve")
        att_done_pe = c.now("pe")
        att_done_dve = c.now("dve")

        if upto < 5:
            return finish(c)
        c.w("pe", "dve", att_done_dve)
        for qc in range(QC):
            layernorm(xhat2, qc)
        ln2_dve = c.now("dve")

        for dc in range(DC):
            for qc in range(QC):
                q0 = qc * 512
                a = dc * ES + q0
                w0 = 1 if qc == 0 else 0
                c.drain("dve")
                c.dve(lambda e, a=a, w0=w0: e.tensor_sub(
                    pM[:, w0:512], xhat2[:, a + w0: a + 512], xhat2[:, a + w0 - 1: a + 511]))
                c.drain("dve")
                c.dve(lambda e, a=a, w0=w0, dc=dc: e.scalar_tensor_tensor(
                    xr[:, a + w0: a + 512], pM[:, w0:512], mur_sb[:, dc:dc + 1],
                    xhat2[:, a + w0 - 1: a + 511], ALU.mult, ALU.add))
                c.dve(lambda e, a=a, w0=w0, dc=dc: e.scalar_tensor_tensor(
                    xk[:, a + w0: a + 512], pM[:, w0:512], muk_sb[:, dc:dc + 1],
                    xhat2[:, a + w0 - 1: a + 511], ALU.mult, ALU.add))
                if qc == 0:
                    c.dve(lambda e, a=a, dc=dc: e.tensor_scalar_mul(
                        xr[:, a:a + 1], xhat2[:, a:a + 1], mur_sb[:, dc:dc + 1]))
                    c.dve(lambda e, a=a, dc=dc: e.tensor_scalar_mul(
                        xk[:, a:a + 1], xhat2[:, a:a + 1], muk_sb[:, dc:dc + 1]))
        blend_dve = c.now("dve")

        if upto < 6:
            return finish(c)
        def wffn_ap(dw, oc):
            return bass.AP(dw, oc * 128, [[D, 128], [128 * D, DC], [1, 128]])

        ffn_pe_mark = [0, 0]
        ffn_lds = [0] * 24
        fi = 0
        for phase, (dw, src, kind) in enumerate([
                (dwr, xr, "r"), (dwk, xk, "k"), (dwv, kact, "v")]):
            for oc in range(8):
                gi = phase * 8 + oc
                if gi == 0:
                    c.w("gp", "pe", qkv_pe_mark[23])
                    c.dma_gp(wst[:, 0:DC * 128], wffn_ap(dwr, 0), "lq0")
                    ffn_lds[0] = c.now("lq0")
                    c.dma_gp(wst[:, DC * 128:], wffn_ap(dwr, 1), "lq1")
                    ffn_lds[1] = c.now("lq1")
                elif gi + 1 < 24:
                    nphase, noc = divmod(gi + 1, 8)
                    ndw = [dwr, dwk, dwv][nphase]
                    c.w("gp", "pe", ffn_pe_mark[(gi - 1) % 2])
                    sl = ((gi + 1) % 2) * DC * 128
                    c.dma_gp(wst[:, sl:sl + DC * 128], wffn_ap(ndw, noc), "lq" + str((gi + 1) % 2))
                    ffn_lds[gi + 1] = c.now("lq" + str((gi + 1) % 2))
                c.w("pe", "lq" + str(gi % 2), ffn_lds[gi])
                if gi == 0:
                    c.w("pe", "dve", blend_dve)
                wsl = (gi % 2) * DC * 128
                for qc in range(QC):
                    q0 = qc * 512
                    bank = (fi % 2) * 512
                    if fi >= 2:
                        am, dm, _pm = evac_mark[fi - 2]
                        c.w("pe", "act", am)
                        c.w("pe", "dve", dm)
                    if kind == "v" and fi == 2 * 8 * QC:
                        c.w("pe", "dve", evac_mark[2 * 8 * QC - 1][1])
                    for dc in range(DC):
                        c.pe(lambda e, dc=dc, q0=q0, bank=bank, wsl=wsl, src=src: e.matmul(
                            pA[:, bank:bank + 512],
                            wst[:, wsl + dc * 128: wsl + dc * 128 + 128],
                            src[:, dc * ES + q0: dc * ES + q0 + 512],
                            start=(dc == 0), stop=(dc == DC - 1)))
                    pe_mm = c.now("pe")
                    if kind == "r":
                        c.w("act", "pe", pe_mm)
                        c.act(lambda e, oc=oc, q0=q0, bank=bank: e.activation(
                            rsig[:, oc * ES + q0: oc * ES + q0 + 512],
                            pA[:, bank:bank + 512], AF.Sigmoid))
                    elif kind == "k":
                        c.w("dve", "pe", pe_mm)
                        c.drain("dve")
                        c.dve(lambda e, bank=bank: e.tensor_scalar_max(
                            sqf[:, 0:512], pA[:, bank:bank + 512], 0.0))
                        c.drain("dve")
                        c.dve(lambda e, oc=oc, q0=q0: e.tensor_mul(
                            kact[:, oc * ES + q0: oc * ES + q0 + 512], sqf[:, 0:512], sqf[:, 0:512]))
                    else:
                        c.w("dve", "pe", pe_mm)
                        c.drain("dve")
                        c.dve(lambda e, oc=oc, q0=q0, bank=bank: e.tensor_mul(
                            sqf[:, 0:512], pA[:, bank:bank + 512],
                            rsig[:, oc * ES + q0: oc * ES + q0 + 512]))
                        c.drain("dve")
                        c.dve(lambda e, oc=oc, q0=q0: e.tensor_add(
                            xs(oc, q0, q0 + 512), xs(oc, q0, q0 + 512), sqf[:, 0:512]))
                    evac_mark[fi] = (c.now("act"), c.now("dve"), c.now("pe"))
                    fi += 1
                ffn_pe_mark[gi % 2] = c.now("pe")
        ffn_dve = c.now("dve")
        ffn_act = c.now("act")

        return finish(c)

    def finish(c):
        c.w("pe", "dve", c.now("dve"))
        c.w("pe", "act", c.now("act"))
        c.w("pe", "gp", c.now("gp"))
        out_dve_mark = []
        for t in range(TT):
            for g in range(2):
                c.w("pe", "dve", c.now("dve"))
                for j in range(4):
                    dcj = g * 4 + j
                    c.pe(lambda e, t=t, dcj=dcj, j=j: e.transpose(
                        pT[:, j * 128: j * 128 + 128],
                        xs(dcj, t * 128, t * 128 + 128), identf[:, :]))
                pe_here = c.now("pe")
                c.w("dve", "pe", pe_here)
                if t >= 2 and g == 0:
                    c.w("dve", "ou" + str(t % 2), 16 * ((t - 2) // 2 + 1))
                for j in range(4):
                    dcj = g * 4 + j
                    c.dve(lambda e, t=t, dcj=dcj, j=j, g=g: e.tensor_copy(
                        outstage[:, (t % 2) * D + (g * 4 + j) * 128:
                                 (t % 2) * D + (g * 4 + j) * 128 + 128],
                        pT[:, j * 128: j * 128 + 128]))
            out_dve_mark.append(c.now("dve"))
            c.w("sp", "dve", out_dve_mark[t])
            c.dma_sp(dy.ap()[t * 128:(t + 1) * 128, :],
                     outstage[:, (t % 2) * D:(t % 2) * D + D], "ou" + str(t % 2))
        c.w("sp", "ou0", c.now("ou0"))
        c.w("sp", "ou1", c.now("ou1"))

    with nc.Block() as block:
        @block.tensor
        def _(eng):
            program(Ctx("pe", eng, sems))

        @block.vector
        def _(eng):
            program(Ctx("dve", eng, sems))

        @block.scalar
        def _(eng):
            program(Ctx("act", eng, sems))

        @block.gpsimd
        def _(eng):
            program(Ctx("gp", eng, sems))

        @block.sync
        def _(eng):
            program(Ctx("sp", eng, sems))

    return nc


_NC_CACHE = {}


def _get_nc(S):
    if S not in _NC_CACHE:
        _NC_CACHE[S] = build_nc(S)
    return _NC_CACHE[S]


def make_inputs(b_x, Wqkv, bqkv, Wo, bo, wq, wk, W_r, W_k, W_v, mu_r, mu_k):
    return {
        "x": np.ascontiguousarray(b_x),
        "wqkvT": np.ascontiguousarray(Wqkv.T),
        "woT": np.ascontiguousarray(Wo.T),
        "wrT": np.ascontiguousarray(W_r.T),
        "wkT": np.ascontiguousarray(W_k.T),
        "wvT": np.ascontiguousarray(W_v.T),
        "bqkv": np.ascontiguousarray(bqkv.reshape(24, 128).T),
        "bo": np.ascontiguousarray(bo.reshape(8, 128).T),
        "mur": np.ascontiguousarray(mu_r.reshape(8, 128).T),
        "muk": np.ascontiguousarray(mu_k.reshape(8, 128).T),
        "wq2": np.ascontiguousarray(np.tile(wq, 2).reshape(128, 1)),
        "wk2": np.ascontiguousarray(np.tile(wk, 2).reshape(128, 1)),
    }


def kernel(L0_attn_x, L0_attn_Wqkv, L0_attn_bqkv, L0_attn_Wo, L0_attn_bo,
           L0_attn_wq, L0_attn_wk, L0_ffn_W_r, L0_ffn_W_k, L0_ffn_W_v,
           L0_ffn_mu_r, L0_ffn_mu_k):
    S = L0_attn_x.shape[1]
    nc = _get_nc(S)
    in_maps = []
    for b in range(NCORES):
        in_maps.append(make_inputs(
            L0_attn_x[b], L0_attn_Wqkv, L0_attn_bqkv, L0_attn_Wo, L0_attn_bo,
            L0_attn_wq, L0_attn_wk, L0_ffn_W_r, L0_ffn_W_k, L0_ffn_W_v,
            L0_ffn_mu_r, L0_ffn_mu_k))
    res = run_bass_kernel_spmd(nc, in_maps, core_ids=list(range(NCORES)))
    return np.stack([res.results[b]["y"] for b in range(NCORES)], axis=0)


# version 04
# speedup vs baseline: 1.0019x; 1.0019x over previous
import contextlib
import numpy as np
import concourse.bass as bass
import concourse.mybir as mybir
from concourse.bass_utils import run_bass_kernel_spmd

F32 = mybir.dt.float32
BF16 = mybir.dt.bfloat16
F32R = mybir.dt.float32r
AF = mybir.ActivationFunctionType
ALU = mybir.AluOpType
AX = mybir.AxisListType.X

D = 1024
DC = 8
H = 16
DH = 64
NCORES = 8


class Ctx:

    def __init__(self, active, eng, sems):
        self.active = active
        self.eng = eng
        self.h = sems
        self.cnt = {k: 0 for k in sems}

    def _fin(self, instr, inc, n=1):
        if inc is not None:
            self.cnt[inc] += n
            if instr is not None:
                instr.then_inc(self.h[inc], n)
        return instr

    def w(self, eng, sem, val):
        if self.active == eng and val > 0:
            self.eng.wait_ge(self.h[sem], val)

    def now(self, sem):
        return self.cnt[sem]

    def drain(self, eng):
        if self.active == eng and self.cnt[eng] > 0:
            self.eng.wait_ge(self.h[eng], self.cnt[eng])

    def pe(self, f):
        return self._fin(f(self.eng) if self.active == "pe" else None, "pe")

    def dve(self, f):
        return self._fin(f(self.eng) if self.active == "dve" else None, "dve")

    def act(self, f):
        return self._fin(f(self.eng) if self.active == "act" else None, "act")

    def gp(self, f):
        return self._fin(f(self.eng) if self.active == "gp" else None, "gp")

    def dma_sp(self, out, in_, sem):
        i = self.eng.dma_start(out=out, in_=in_) if self.active == "sp" else None
        return self._fin(i, sem, 16)

    def dma_gp(self, out, in_, sem):
        i = self.eng.dma_start(out=out, in_=in_) if self.active == "gp" else None
        return self._fin(i, sem, 16)


def build_nc(S=2048, upto=9):
    KC = S // 128
    QC = S // 512
    TT = S // 128
    nc = bass.Bass("TRN2", target_bir_lowering=False)

    dx = nc.dram_tensor("x", [S, D], F32, kind="ExternalInput")
    dwqkv = nc.dram_tensor("wqkvT", [D, 3 * D], F32, kind="ExternalInput")
    dwo = nc.dram_tensor("woT", [D, D], F32, kind="ExternalInput")
    dwr = nc.dram_tensor("wrT", [D, D], F32, kind="ExternalInput")
    dwk = nc.dram_tensor("wkT", [D, D], F32, kind="ExternalInput")
    dwv = nc.dram_tensor("wvT", [D, D], F32, kind="ExternalInput")
    dbqkv = nc.dram_tensor("bqkv", [128, 24], F32, kind="ExternalInput")
    dbo = nc.dram_tensor("bo", [128, 8], F32, kind="ExternalInput")
    dmur = nc.dram_tensor("mur", [128, 8], F32, kind="ExternalInput")
    dmuk = nc.dram_tensor("muk", [128, 8], F32, kind="ExternalInput")
    dwq2 = nc.dram_tensor("wq2", [128, 1], F32, kind="ExternalInput")
    dwk2 = nc.dram_tensor("wk2", [128, 1], F32, kind="ExternalInput")
    dy = nc.dram_tensor("y", [S, D], F32, kind="ExternalOutput")

    BASE = (nc.sbuf_base + 31) // 32 * 32
    off = [BASE]

    def alloc(name, shape, dtype, at=None):
        per = int(np.prod(shape[1:])) * mybir.dt.size(dtype)
        o = off[0] if at is None else at
        t = nc.alloc_sbuf_tensor_at(name, shape, dtype, offset=o)
        if at is None:
            off[0] = o + (per + 31) // 32 * 32
        return t

    ES = S
    x_sb = alloc("x_sb", [128, DC * S], F32)
    o_qt = off[0]
    qt = alloc("qt", [128, DC * S], BF16)
    xk = alloc("xk", [128, DC * S], BF16, at=o_qt)
    o_kt = off[0]
    kt = alloc("kt", [128, DC * S], BF16)
    rsig = alloc("rsig", [128, DC * S], BF16, at=o_kt)
    o_vt = off[0]
    vt = alloc("vt", [128, KC * H * (DH + 1)], BF16)
    xhat2 = alloc("xhat2", [128, DC * S], BF16, at=o_vt)
    kact = alloc("kact", [128, DC * S], BF16, at=o_vt)
    o_E = off[0]
    E_BYTES = max(2 * DC * S, 4096 + DC * 512 * 2 + 4 * DC * 128 * 2, 2 * D * 4)
    xhat = alloc("xhat", [128, DC * S], BF16, at=o_E)
    off[0] = o_E + E_BYTES
    instage = alloc("instage", [128, 2 * D], F32, at=o_E)
    outstage = alloc("outstage", [128, 2 * D], F32, at=o_E)
    xr = alloc("xr", [128, DC * S], BF16, at=o_E)
    attn = alloc("attn", [128, 2 * 2 * 512], BF16, at=o_E)
    zs = alloc("zs", [128, DC * 512], BF16, at=o_E + 4096)
    wos = alloc("wos", [128, 4 * DC * 128], BF16, at=o_E + 4096 + 8192)
    wst = alloc("wst", [128, 2 * DC * 128], BF16)
    tq = alloc("tq", [128, 512], F32)
    sqb = alloc("sqb", [128, 512], BF16)
    xbt = alloc("xbt", [128, 512], BF16)
    xqt = alloc("xqt", [128, 512], BF16)
    stats = alloc("stats", [128, 512], F32)
    statr = alloc("statr", [128, 512], F32R)
    small = alloc("small", [128, 32], F32)
    aones_b = alloc("aones_b", [128, 1], BF16)
    honesA_b = alloc("honesA_b", [128, 1], BF16)
    honesB_b = alloc("honesB_b", [128, 1], BF16)
    bqkv_sb = alloc("bqkv_sb", [128, 24], F32)
    bo_sb = alloc("bo_sb", [128, 8], F32)
    mur_sb = alloc("mur_sb", [128, 8], F32)
    muk_sb = alloc("muk_sb", [128, 8], F32)
    wq2_sb = alloc("wq2_sb", [128, 1], F32)
    wk2_sb = alloc("wk2_sb", [128, 1], F32)
    identf = alloc("identf", [128, 128], F32)
    rowsr = alloc("rowsr", [128, 256], F32R)
    nb = alloc("nb", [128, 8], F32)
    nbr = alloc("nbr", [128, 8], F32R)
    norm_acc = alloc("norm_acc", [128, 72], F32)
    csts = alloc("csts", [128, 8], F32)
    assert off[0] <= 229344, f"sbuf overflow: {off[0]}"
    nc.alloc_sbuf_tensor("reserve_manual", [128, off[0] - BASE], mybir.dt.uint8)
    rowsf = tq
    lrec = stats
    lrec_r = statr
    rz_sb = tq
    sqf = tq

    pA = nc.alloc_psum_tensor("pA", [128, 1024], F32)
    pB = nc.alloc_psum_tensor("pB", [128, 1024], F32)
    pZ = nc.alloc_psum_tensor("pZ", [128, 1024], F32)
    pM = nc.alloc_psum_tensor("pM", [128, 512], F32)
    pT = nc.alloc_psum_tensor("pT", [128, 512], F32)

    nc.const_aps.register(F32, 1e-5, csts[:, 0:1])
    nc.const_aps.register(F32, 1e-6, csts[:, 1:2])
    nc.const_aps.register(F32, 1.0 / DH, csts[:, 2:3])
    nc.const_aps.register(F32, -0.5, csts[:, 5:6])
    nc.const_aps.register(F32, -1.0, csts[:, 6:7])

    sem_names = ["pe", "dve", "act", "gp", "ldp", "lq0", "lq1",
                 "wo0", "wo1", "wo2", "wo3", "ldh0", "ldh1", "ou0", "ou1"]
    cm = contextlib.ExitStack()
    sems = {n: cm.enter_context(nc.semaphore("s_" + n)) for n in sem_names}

    def xs(dc, a, b):
        return x_sb[:, dc * ES + a: dc * ES + b]

    def program(c: Ctx):
        c.gp(lambda e: e.iota(identf[:, :], [[-1, 128]], channel_multiplier=1,
                              allow_small_or_imprecise_dtypes=True))
        c.gp(lambda e: e.memset(rowsf[:, :], 0.0))
        c.drain("gp")
        for _r in (0, 32, 64):
            c.gp(lambda e, _r=_r: e.memset(rowsf[_r:_r + 1, 0:128], 1.0))
            c.drain("gp")
        for _r in (0, 64):
            c.gp(lambda e, _r=_r: e.memset(rowsf[_r:_r + 1, 128:192], 1.0))
            c.drain("gp")
        c.gp(lambda e: e.memset(rowsf[32:33, 192:256], 1.0))
        c.drain("gp")
        c.gp(lambda e: e.memset(aones_b[:, :], 1.0))
        c.drain("gp")
        c.gp(lambda e: e.memset(honesA_b[:, :], 0.0))
        c.drain("gp")
        c.gp(lambda e: e.memset(honesA_b[0:64, :], 1.0))
        c.drain("gp")
        c.gp(lambda e: e.memset(honesB_b[:, :], 0.0))
        c.drain("gp")
        c.gp(lambda e: e.memset(honesB_b[64:128, :], 1.0))
        c.drain("gp")
        c.gp(lambda e: e.memset(vt[:, :], 1.0))
        c.drain("gp")
        c.gp(lambda e: e.memset(csts[:, 0:1], 1e-5))
        c.drain("gp")
        c.gp(lambda e: e.memset(csts[:, 1:2], 1e-6))
        c.drain("gp")
        c.gp(lambda e: e.memset(csts[:, 2:3], 1.0 / DH))
        c.drain("gp")
        c.gp(lambda e: e.memset(csts[:, 3:4], 0.0))
        c.drain("gp")
        c.gp(lambda e: e.memset(csts[:, 4:5], 1.0))
        c.drain("gp")
        c.gp(lambda e: e.memset(csts[:, 5:6], -0.5))
        c.drain("gp")
        c.gp(lambda e: e.memset(csts[:, 6:7], -1.0))
        c.drain("gp")
        gp_const = c.now("gp")

        c.dma_gp(bqkv_sb[:, :], dbqkv.ap(), "ldp")
        c.dma_gp(bo_sb[:, :], dbo.ap(), "ldp")
        c.dma_gp(mur_sb[:, :], dmur.ap(), "ldp")
        c.dma_gp(muk_sb[:, :], dmuk.ap(), "ldp")
        c.dma_gp(wq2_sb[:, :], dwq2.ap(), "ldp")
        c.dma_gp(wk2_sb[:, :], dwk2.ap(), "ldp")
        lds_params = c.now("ldp")

        c.w("dve", "gp", gp_const)
        c.dve(lambda e: e.tensor_single_scalar(identf[:, :], identf[:, :], 0.0, ALU.is_equal))
        c.dve(lambda e: e.tensor_copy(rowsr[:, :], rowsf[:, 0:256]))
        c.w("dve", "ldp", lds_params)
        c.dve(lambda e: e.tensor_mul(small[:, 0:1], wq2_sb[:, :], wk2_sb[:, :]))
        dve_const = c.now("dve")

        dve_tile_mark = []
        for t in range(TT):
            if t >= 2:
                c.w("sp", "dve", dve_tile_mark[t - 2])
            hsem = "ldh" + str(t % 2)
            c.dma_sp(instage[:, (t % 2) * D:(t % 2) * D + D],
                     dx.ap()[t * 128:(t + 1) * 128, :], hsem)
            c.w("pe", hsem, c.now(hsem))
            if t == 0:
                c.w("pe", "dve", dve_const)
            for g in range(2):
                if t > 0 or g > 0:
                    c.w("pe", "dve", c.now("dve"))
                for j in range(4):
                    dcj = g * 4 + j
                    c.pe(lambda e, t=t, dcj=dcj, j=j: e.transpose(
                        pT[:, j * 128: j * 128 + 128],
                        instage[:, (t % 2) * D + dcj * 128:(t % 2) * D + dcj * 128 + 128],
                        identf[:, :]))
                pe_here = c.now("pe")
                c.w("dve", "pe", pe_here)
                for j in range(4):
                    dcj = g * 4 + j
                    c.dve(lambda e, t=t, dcj=dcj, j=j: e.tensor_copy(
                        xs(dcj, t * 128, t * 128 + 128),
                        pT[:, j * 128: j * 128 + 128]))
            dve_tile_mark.append(c.now("dve"))
        x_ready = c.now("dve")

        if upto < 1:
            return finish(c)
        def layernorm(dst, qc):
            q0 = qc * 512
            for dc in range(DC):
                c.w("dve", "pe", c.now("pe"))
                c.dve(lambda e, dc=dc, q0=q0: e.tensor_copy(
                    xbt[:, :], xs(dc, q0, q0 + 512)))
                c.dve(lambda e, dc=dc, q0=q0: e.tensor_mul(
                    xqt[:, :], xs(dc, q0, q0 + 512), xs(dc, q0, q0 + 512)))
                d_here = c.now("dve")
                c.w("pe", "dve", d_here)
                c.pe(lambda e, dc=dc: e.matmul(
                    pB[0:1, 0:512], aones_b[:, :], xbt[:, :],
                    start=(dc == 0), stop=(dc == DC - 1), tile_position=(0, 0)))
                c.pe(lambda e, dc=dc: e.matmul(
                    pB[32:33, 512:1024], aones_b[:, :], xqt[:, :],
                    start=(dc == 0), stop=(dc == DC - 1), tile_position=(0, 32)))
            pe_stats = c.now("pe")
            c.w("dve", "pe", pe_stats)
            c.dve(lambda e: e.tensor_copy(statr[0:1, :], pB[0:1, 0:512]))
            c.dve(lambda e: e.tensor_copy(statr[32:33, :], pB[32:33, 512:1024]))
            dve_sr = c.now("dve")
            c.w("pe", "dve", dve_sr)
            c.pe(lambda e: e.matmul(pM[:, 0:512], rowsr[0:1, 0:128], statr[0:1, :],
                                    start=True, stop=True))
            c.pe(lambda e: e.matmul(pB[:, 0:512], rowsr[32:33, 0:128], statr[32:33, :],
                                    start=True, stop=True))
            pe_bc = c.now("pe")
            c.w("dve", "pe", pe_bc)
            c.drain("dve")
            c.dve(lambda e: e.tensor_scalar(tq[:, 0:512], pM[:, 0:512],
                                            1.0 / D, None, ALU.mult))
            c.drain("dve")
            c.dve(lambda e: e.tensor_mul(stats[:, :], tq[:, 0:512], tq[:, 0:512]))
            c.drain("dve")
            c.dve(lambda e: e.scalar_tensor_tensor(stats[:, :], pB[:, 0:512],
                                                   1.0 / D, stats[:, :],
                                                   ALU.mult, ALU.subtract))
            dve_v = c.now("dve")
            c.w("act", "dve", dve_v)
            c.act(lambda e: e.activation(pM[:, 0:512], stats[:, :], AF.Sqrt,
                                         bias=1e-5, scale=1.0))
            act_sd = c.now("act")
            c.w("dve", "act", act_sd)
            c.dve(lambda e: e.reciprocal(stats[:, :], pM[:, 0:512]))
            c.drain("dve")
            c.dve(lambda e: e.scalar_tensor_tensor(pM[:, 0:512], tq[:, 0:512], -1.0,
                                                   stats[:, :], ALU.mult, ALU.mult))
            c.drain("dve")
            dve_ab = c.now("dve")
            for dc in range(DC):
                c.dve(lambda e, dc=dc, q0=q0: e.tensor_mul(
                    tq[:, 0:512], xs(dc, q0, q0 + 512), stats[:, :]))
                c.drain("dve")
                c.dve(lambda e, dc=dc, q0=q0: e.tensor_add(
                    dst[:, dc * ES + q0: dc * ES + q0 + 512], tq[:, 0:512],
                    pM[:, 0:512]))
                c.drain("dve")
            return c.now("pe"), c.now("dve")

        c.w("pe", "dve", x_ready)
        for qc in range(QC):
            layernorm(xhat, qc)
        ln1_pe, ln1_dve = c.now("pe"), c.now("dve")

        if upto < 2:
            return finish(c)
        def wqkv_ap(oc):
            return bass.AP(dwqkv, oc * 128, [[3 * D, 128], [128 * 3 * D, DC], [1, 128]])

        qkv_pe_mark = [0] * 24
        lds_mark = [0] * 24
        evac_mark = [(0, 0, 0)] * (24 * QC)
        chain = 0
        for oc in range(24):
            if oc == 0:
                c.dma_gp(wst[:, 0:DC * 128], wqkv_ap(0), "lq0")
                lds_mark[0] = c.now("lq0")
                c.dma_gp(wst[:, DC * 128:2 * DC * 128], wqkv_ap(1), "lq1")
                lds_mark[1] = c.now("lq1")
            elif oc + 1 < 24:
                c.w("gp", "pe", qkv_pe_mark[oc - 1])
                sl = ((oc + 1) % 2) * DC * 128
                c.dma_gp(wst[:, sl:sl + DC * 128], wqkv_ap(oc + 1), "lq" + str((oc + 1) % 2))
                lds_mark[oc + 1] = c.now("lq" + str((oc + 1) % 2))
            c.w("pe", "lq" + str(oc % 2), lds_mark[oc])
            if oc == 0:
                c.w("pe", "dve", ln1_dve)
            wsl = (oc % 2) * DC * 128
            for qc in range(QC):
                q0 = qc * 512
                bank = (chain % 2) * 512
                if chain >= 2:
                    am, dm, _pm = evac_mark[chain - 2]
                    c.w("pe", "act", am)
                    c.w("pe", "dve", dm)
                for dc in range(DC):
                    c.pe(lambda e, dc=dc, q0=q0, bank=bank, wsl=wsl: e.matmul(
                        pA[:, bank:bank + 512],
                        wst[:, wsl + dc * 128: wsl + dc * 128 + 128],
                        xhat[:, dc * ES + q0: dc * ES + q0 + 512],
                        start=(dc == 0), stop=(dc == DC - 1)))
                pe_mm = c.now("pe")
                tsl = 0
                c.w("act", "pe", pe_mm)
                if chain >= 1:
                    c.w("act", "dve", evac_mark[chain - 1][1])
                    c.w("act", "pe", evac_mark[chain - 1][2])
                c.act(lambda e, oc=oc, bank=bank, tsl=tsl: e.activation(
                    tq[:, tsl:tsl + 512], pA[:, bank:bank + 512], AF.Identity,
                    bias=bqkv_sb[:, oc:oc + 1], scale=1.0))
                act_t = c.now("act")
                if oc < 16:
                    pair = oc % 8
                    c.w("dve", "act", act_t)
                    c.dve(lambda e, tsl=tsl: e.tensor_mul(
                        sqb[:, :], tq[:, tsl:tsl + 512], tq[:, tsl:tsl + 512]))
                    dve_sq = c.now("dve")
                    c.w("pe", "dve", dve_sq)
                    c.pe(lambda e: e.matmul(pB[0:1, 0:512], honesA_b[:, :], sqb[:, :],
                                            start=True, stop=True, tile_position=(0, 0)))
                    c.pe(lambda e: e.matmul(pB[32:33, 512:1024], honesB_b[:, :], sqb[:, :],
                                            start=True, stop=True, tile_position=(0, 32)))
                    pe_rms = c.now("pe")
                    c.w("dve", "pe", pe_rms)
                    c.drain("dve")
                    c.dve(lambda e: e.tensor_copy(statr[0:1, :], pB[0:1, 0:512]))
                    c.dve(lambda e: e.tensor_copy(statr[32:33, :], pB[32:33, 512:1024]))
                    dve_sr2 = c.now("dve")
                    c.w("pe", "dve", dve_sr2)
                    c.pe(lambda e: e.matmul(pB[:, 0:512], rowsr[0:1, 0:128], statr[0:1, :],
                                            start=True, stop=True))
                    c.pe(lambda e: e.matmul(pB[:, 512:1024], rowsr[32:33, 0:128], statr[32:33, :],
                                            start=True, stop=True))
                    pe_ssq = c.now("pe")
                    c.w("act", "pe", pe_ssq)
                    c.act(lambda e: e.activation(pM[0:64, 0:512], pB[0:64, 0:512], AF.Sqrt,
                                                 bias=1e-6, scale=1.0 / DH))
                    c.act(lambda e: e.activation(pM[64:128, 0:512], pB[64:128, 512:1024], AF.Sqrt,
                                                 bias=csts[64:128, 1:2], scale=csts[64:128, 2:3]))
                    act_sd2 = c.now("act")
                    c.w("dve", "act", act_sd2)
                    c.dve(lambda e: e.reciprocal(stats[:, :], pM[:, 0:512]))
                    c.drain("dve")
                    dve_rsq = c.now("dve")
                    if oc < 8:
                        c.dve(lambda e, pair=pair, q0=q0, tsl=tsl: e.scalar_tensor_tensor(
                            qt[:, pair * ES + q0: pair * ES + q0 + 512],
                            tq[:, tsl:tsl + 512], small[:, 0:1], stats[:, :],
                            ALU.mult, ALU.mult))
                    else:
                        c.dve(lambda e, pair=pair, q0=q0, tsl=tsl: e.tensor_mul(
                            kt[:, pair * ES + q0: pair * ES + q0 + 512],
                            tq[:, tsl:tsl + 512], stats[:, :]))
                    c.drain("dve")
                    dstt = qt if oc < 8 else kt
                    c.dve(lambda e, pair=pair, q0=q0, dstt=dstt: e.tensor_mul(
                        sqb[:, :], dstt[:, pair * ES + q0: pair * ES + q0 + 512],
                        dstt[:, pair * ES + q0: pair * ES + q0 + 512]))
                    dve_csq = c.now("dve")
                    c.w("pe", "dve", dve_csq)
                    c.pe(lambda e: e.matmul(pM[0:1, 0:512], honesA_b[:, :], sqb[:, :],
                                            start=True, stop=True, tile_position=(0, 0)))
                    c.pe(lambda e: e.matmul(pM[32:33, 0:512], honesB_b[:, :], sqb[:, :],
                                            start=True, stop=True, tile_position=(0, 32)))
                    pe_cs = c.now("pe")
                    col = (0 if oc < 8 else 36) + pair * QC + qc
                    c.w("dve", "pe", pe_cs)
                    c.dve(lambda e, col=col: e.reduce_max(
                        norm_acc[0:1, col:col + 1], pM[0:1, 0:512], axis=AX))
                    c.dve(lambda e, col=col: e.reduce_max(
                        norm_acc[32:33, col:col + 1], pM[32:33, 0:512], axis=AX))
                else:
                    vc = oc - 16
                    c.w("pe", "act", act_t)
                    if chain >= 1:
                        c.w("pe", "dve", c.now("dve"))
                    for tb in range(4):
                        c.pe(lambda e, tb=tb, tsl=tsl: e.transpose(
                            pT[:, tb * 128:tb * 128 + 128],
                            tq[:, tsl + tb * 128: tsl + tb * 128 + 128], identf[:, :]))
                    pe_vt = c.now("pe")
                    c.w("dve", "pe", pe_vt)
                    for tb in range(4):
                        kc = qc * 4 + tb
                        base = kc * H * (DH + 1) + vc * 2 * (DH + 1)
                        c.dve(lambda e, tb=tb, base=base: e.tensor_copy(
                            bass.AP(vt, base, [[KC * H * (DH + 1), 128], [DH + 1, 2], [1, DH]]),
                            pT[:, tb * 128:tb * 128 + 128].rearrange("p (g n) -> p g n", g=2)))
                evac_mark[chain] = (c.now("act"), c.now("dve"), c.now("pe"))
                chain += 1
            qkv_pe_mark[oc] = c.now("pe")
        qkv_pe = c.now("pe")

        if upto < 3:
            return finish(c)
        c.w("dve", "pe", qkv_pe)
        c.drain("dve")
        for pair in range(8):
            qa, ka = pair * QC, 36 + pair * QC
            for r in (0, 32):
                c.dve(lambda e, qa=qa, r=r: e.reduce_max(
                    small[r:r + 1, 2:3], norm_acc[r:r + 1, qa:qa + QC], axis=AX))
                c.dve(lambda e, ka=ka, r=r: e.reduce_max(
                    small[r:r + 1, 3:4], norm_acc[r:r + 1, ka:ka + QC], axis=AX))
                c.drain("dve")
                c.dve(lambda e, r=r: e.tensor_mul(small[r:r + 1, 4:5],
                                                  small[r:r + 1, 2:3], small[r:r + 1, 3:4]))
                c.drain("dve")
            dve_cs2 = c.now("dve")
            c.w("act", "dve", dve_cs2)
            c.act(lambda e: e.activation(small[0:1, 5:6], small[0:1, 4:5], AF.Sqrt))
            c.act(lambda e: e.activation(small[32:33, 5:6], small[32:33, 4:5], AF.Sqrt,
                                         bias=csts[32:33, 3:4], scale=csts[32:33, 4:5]))
            act_cs = c.now("act")
            c.w("dve", "act", act_cs)
            for r in (0, 32):
                c.dve(lambda e, pair=pair, r=r: e.tensor_scalar(
                    nb[r:r + 1, pair:pair + 1], small[r:r + 1, 5:6],
                    -1.0, 71.0, ALU.mult, ALU.add))
                c.drain("dve")
                c.dve(lambda e, pair=pair, r=r: e.tensor_copy(
                    nbr[r:r + 1, pair:pair + 1], nb[r:r + 1, pair:pair + 1]))
                c.drain("dve")
        dve_nb = c.now("dve")
        c.w("pe", "dve", dve_nb)
        c.pe(lambda e: e.matmul(pM[:, 0:8], rowsr[0:1, 0:128],
                                nbr[0:1, 0:8], start=True, stop=True))
        c.pe(lambda e: e.matmul(pM[:, 8:16], rowsr[32:33, 0:128],
                                nbr[32:33, 0:8], start=True, stop=True))
        pe_nb = c.now("pe")
        c.w("dve", "pe", pe_nb)
        for pair in range(8):
            for hh in range(2):
                c.dve(lambda e, pair=pair, hh=hh: e.tensor_copy(
                    small[:, 16 + 2 * pair + hh:17 + 2 * pair + hh],
                    pM[:, hh * 8 + pair:hh * 8 + pair + 1]))
        dve_bias = c.now("dve")

        if upto < 4:
            return finish(c)
        last_exp = {0: 0, 1: 0}
        last_z = {0: 0, 1: 0}
        wo_dve_mark = 0
        wo_pe_mark = [0] * 4
        lds_wo = [0] * (QC * 8)
        first_att = True

        zevac_dve = 0
        for qc in range(QC):
            q0 = qc * 512
            for pair in range(8):
                if first_att:
                    c.w("pe", "dve", dve_bias)
                    first_att = False
                if pair == 0 and qc > 0:
                    c.w("pe", "dve", wo_dve_mark)
                if pair < 4:
                    toc = pair
                    slot = toc % 4
                    if qc == 0:
                        c.w("gp", "pe", qkv_pe)
                    c.w("gp", "pe", wo_pe_mark[slot])
                    c.dma_gp(wos[:, slot * DC * 128:(slot + 1) * DC * 128],
                             bass.AP(dwo, toc * 128, [[D, 128], [128 * D, DC], [1, 128]]),
                             "wo" + str(slot))
                    lds_wo[qc * 8 + toc] = c.now("wo" + str(slot))

                zm = {}
                for kc in range(KC):
                    sl = kc % 2
                    c.w("pe", "act", last_exp[sl])
                    c.pe(lambda e, pair=pair, kc=kc, sl=sl, q0=q0: e.matmul(
                        pA[:, sl * 512: sl * 512 + 512],
                        kt[0:64, pair * ES + kc * 128: pair * ES + kc * 128 + 128],
                        qt[0:64, pair * ES + q0: pair * ES + q0 + 512],
                        start=True, stop=True, tile_position=(0, 0)))
                    c.pe(lambda e, pair=pair, kc=kc, sl=sl, q0=q0: e.matmul(
                        pB[:, sl * 512: sl * 512 + 512],
                        kt[64:128, pair * ES + kc * 128: pair * ES + kc * 128 + 128],
                        qt[64:128, pair * ES + q0: pair * ES + q0 + 512],
                        start=True, stop=True, tile_position=(64, 0)))
                    sc_pe = c.now("pe")
                    c.w("act", "pe", max(sc_pe, last_z[sl]))
                    c.act(lambda e, pair=pair, sl=sl: e.activation(
                        attn[:, (0 * 2 + sl) * 512:(0 * 2 + sl) * 512 + 512],
                        pA[:, sl * 512: sl * 512 + 512], AF.Exp,
                        bias=small[:, 16 + 2 * pair:17 + 2 * pair], scale=1.0))
                    c.act(lambda e, pair=pair, sl=sl: e.activation(
                        attn[:, (1 * 2 + sl) * 512:(1 * 2 + sl) * 512 + 512],
                        pB[:, sl * 512: sl * 512 + 512], AF.Exp,
                        bias=small[:, 17 + 2 * pair:18 + 2 * pair], scale=1.0))
                    last_exp[sl] = c.now("act")
                    c.w("pe", "act", last_exp[sl])
                    if kc == 0:
                        c.w("pe", "dve", zevac_dve)
                    for hh in range(2):
                        hl = 2 * pair + hh
                        c.pe(lambda e, hh=hh, hl=hl, kc=kc, sl=sl: e.matmul(
                            pZ[0:65, hh * 512: hh * 512 + 512],
                            vt[:, kc * H * (DH + 1) + hl * (DH + 1):
                               kc * H * (DH + 1) + hl * (DH + 1) + DH + 1],
                            attn[:, (hh * 2 + sl) * 512:(hh * 2 + sl) * 512 + 512],
                            start=(kc == 0), stop=(kc == KC - 1)))
                    last_z[sl] = c.now("pe")
                z_pe = c.now("pe")
                for hh in range(2):
                    c.w("dve", "pe", z_pe)
                    c.dve(lambda e, hh=hh: e.reciprocal(lrec[0:1, 0:512],
                                                        pZ[64:65, hh * 512: hh * 512 + 512]))
                    c.drain("dve")
                    c.dve(lambda e: e.tensor_copy(statr[0:1, 0:512], lrec[0:1, 0:512]))
                    dve_lr = c.now("dve")
                    c.w("pe", "dve", dve_lr)
                    c.pe(lambda e: e.matmul(pM[0:64, 0:512], rowsr[0:1, 128:192],
                                            statr[0:1, 0:512], start=True, stop=True))
                    pe_rz = c.now("pe")
                    c.w("dve", "pe", pe_rz)
                    c.dve(lambda e: e.tensor_copy(rz_sb[0:64, 0:512], pM[0:64, 0:512]))
                    c.drain("dve")
                    r0 = hh * 64
                    c.dve(lambda e, hh=hh, pair=pair, r0=r0: e.tensor_mul(
                        zs[r0:r0 + 64, pair * 512:(pair + 1) * 512],
                        pZ[0:64, hh * 512: hh * 512 + 512], rz_sb[0:64, 0:512]))
                zevac_dve = c.now("dve")
            zs_dve = c.now("dve")
            att_act = c.now("act")
            wo_evac = 0
            for oc in range(8):
                bank = (oc % 2) * 512
                c.w("pe", "wo" + str(oc % 4), lds_wo[qc * 8 + oc])
                if oc == 0:
                    c.w("pe", "dve", zs_dve)
                    c.w("pe", "act", att_act)
                if oc >= 2:
                    c.w("pe", "dve", c.now("dve"))
                slot = oc % 4
                for dc in range(DC):
                    c.pe(lambda e, oc=oc, dc=dc, bank=bank, slot=slot: e.matmul(
                        pA[:, bank:bank + 512],
                        wos[:, slot * DC * 128 + dc * 128: slot * DC * 128 + dc * 128 + 128],
                        zs[:, dc * 512:(dc + 1) * 512],
                        start=(dc == 0), stop=(dc == DC - 1)))
                wo_pe_mark[slot] = c.now("pe")
                pe_wo = c.now("pe")
                if oc + 4 < 8:
                    toc = oc + 4
                    c.w("gp", "pe", wo_pe_mark[slot])
                    c.dma_gp(wos[:, slot * DC * 128:(slot + 1) * DC * 128],
                             bass.AP(dwo, toc * 128, [[D, 128], [128 * D, DC], [1, 128]]),
                             "wo" + str(slot))
                    lds_wo[qc * 8 + toc] = c.now("wo" + str(slot))
                c.w("dve", "pe", pe_wo)
                c.dve(lambda e, oc=oc, q0=q0, bank=bank: e.scalar_tensor_tensor(
                    xs(oc, q0, q0 + 512), pA[:, bank:bank + 512], bo_sb[:, oc:oc + 1],
                    xs(oc, q0, q0 + 512), ALU.add, ALU.add))
            wo_dve_mark = c.now("dve")
        att_done_pe = c.now("pe")
        att_done_dve = c.now("dve")

        if upto < 5:
            return finish(c)
        c.w("pe", "dve", att_done_dve)
        for qc in range(QC):
            layernorm(xhat2, qc)
        ln2_dve = c.now("dve")

        for dc in range(DC):
            for qc in range(QC):
                q0 = qc * 512
                a = dc * ES + q0
                w0 = 1 if qc == 0 else 0
                c.drain("dve")
                c.dve(lambda e, a=a, w0=w0: e.tensor_sub(
                    pM[:, w0:512], xhat2[:, a + w0: a + 512], xhat2[:, a + w0 - 1: a + 511]))
                c.drain("dve")
                c.dve(lambda e, a=a, w0=w0, dc=dc: e.scalar_tensor_tensor(
                    xr[:, a + w0: a + 512], pM[:, w0:512], mur_sb[:, dc:dc + 1],
                    xhat2[:, a + w0 - 1: a + 511], ALU.mult, ALU.add))
                c.dve(lambda e, a=a, w0=w0, dc=dc: e.scalar_tensor_tensor(
                    xk[:, a + w0: a + 512], pM[:, w0:512], muk_sb[:, dc:dc + 1],
                    xhat2[:, a + w0 - 1: a + 511], ALU.mult, ALU.add))
                if qc == 0:
                    c.dve(lambda e, a=a, dc=dc: e.tensor_scalar_mul(
                        xr[:, a:a + 1], xhat2[:, a:a + 1], mur_sb[:, dc:dc + 1]))
                    c.dve(lambda e, a=a, dc=dc: e.tensor_scalar_mul(
                        xk[:, a:a + 1], xhat2[:, a:a + 1], muk_sb[:, dc:dc + 1]))
        blend_dve = c.now("dve")

        if upto < 6:
            return finish(c)
        def wffn_ap(dw, oc):
            return bass.AP(dw, oc * 128, [[D, 128], [128 * D, DC], [1, 128]])

        ffn_pe_mark = [0, 0]
        ffn_lds = [0] * 24
        fi = 0
        for phase, (dw, src, kind) in enumerate([
                (dwr, xr, "r"), (dwk, xk, "k"), (dwv, kact, "v")]):
            for oc in range(8):
                gi = phase * 8 + oc
                if gi == 0:
                    c.w("gp", "pe", qkv_pe_mark[23])
                    c.dma_gp(wst[:, 0:DC * 128], wffn_ap(dwr, 0), "lq0")
                    ffn_lds[0] = c.now("lq0")
                    c.dma_gp(wst[:, DC * 128:], wffn_ap(dwr, 1), "lq1")
                    ffn_lds[1] = c.now("lq1")
                elif gi + 1 < 24:
                    nphase, noc = divmod(gi + 1, 8)
                    ndw = [dwr, dwk, dwv][nphase]
                    c.w("gp", "pe", ffn_pe_mark[(gi - 1) % 2])
                    sl = ((gi + 1) % 2) * DC * 128
                    c.dma_gp(wst[:, sl:sl + DC * 128], wffn_ap(ndw, noc), "lq" + str((gi + 1) % 2))
                    ffn_lds[gi + 1] = c.now("lq" + str((gi + 1) % 2))
                c.w("pe", "lq" + str(gi % 2), ffn_lds[gi])
                if gi == 0:
                    c.w("pe", "dve", blend_dve)
                wsl = (gi % 2) * DC * 128
                for qc in range(QC):
                    q0 = qc * 512
                    bank = (fi % 2) * 512
                    if fi >= 2:
                        am, dm, _pm = evac_mark[fi - 2]
                        c.w("pe", "act", am)
                        c.w("pe", "dve", dm)
                    if kind == "v" and fi == 2 * 8 * QC:
                        c.w("pe", "dve", evac_mark[2 * 8 * QC - 1][1])
                    for dc in range(DC):
                        c.pe(lambda e, dc=dc, q0=q0, bank=bank, wsl=wsl, src=src: e.matmul(
                            pA[:, bank:bank + 512],
                            wst[:, wsl + dc * 128: wsl + dc * 128 + 128],
                            src[:, dc * ES + q0: dc * ES + q0 + 512],
                            start=(dc == 0), stop=(dc == DC - 1)))
                    pe_mm = c.now("pe")
                    if kind == "r":
                        c.w("act", "pe", pe_mm)
                        c.act(lambda e, oc=oc, q0=q0, bank=bank: e.activation(
                            rsig[:, oc * ES + q0: oc * ES + q0 + 512],
                            pA[:, bank:bank + 512], AF.Sigmoid))
                    elif kind == "k":
                        c.w("dve", "pe", pe_mm)
                        c.drain("dve")
                        c.dve(lambda e, bank=bank: e.tensor_scalar_max(
                            sqf[:, 0:512], pA[:, bank:bank + 512], 0.0))
                        c.drain("dve")
                        c.dve(lambda e, oc=oc, q0=q0: e.tensor_mul(
                            kact[:, oc * ES + q0: oc * ES + q0 + 512], sqf[:, 0:512], sqf[:, 0:512]))
                    else:
                        c.w("dve", "pe", pe_mm)
                        c.drain("dve")
                        c.dve(lambda e, oc=oc, q0=q0, bank=bank: e.tensor_mul(
                            sqf[:, 0:512], pA[:, bank:bank + 512],
                            rsig[:, oc * ES + q0: oc * ES + q0 + 512]))
                        c.drain("dve")
                        c.dve(lambda e, oc=oc, q0=q0: e.tensor_add(
                            xs(oc, q0, q0 + 512), xs(oc, q0, q0 + 512), sqf[:, 0:512]))
                    evac_mark[fi] = (c.now("act"), c.now("dve"), c.now("pe"))
                    fi += 1
                ffn_pe_mark[gi % 2] = c.now("pe")
        ffn_dve = c.now("dve")
        ffn_act = c.now("act")

        return finish(c)

    def finish(c):
        c.w("pe", "dve", c.now("dve"))
        c.w("pe", "act", c.now("act"))
        c.w("pe", "gp", c.now("gp"))
        out_dve_mark = []
        for t in range(TT):
            for g in range(2):
                c.w("pe", "dve", c.now("dve"))
                for j in range(4):
                    dcj = g * 4 + j
                    c.pe(lambda e, t=t, dcj=dcj, j=j: e.transpose(
                        pT[:, j * 128: j * 128 + 128],
                        xs(dcj, t * 128, t * 128 + 128), identf[:, :]))
                pe_here = c.now("pe")
                c.w("dve", "pe", pe_here)
                if t >= 2 and g == 0:
                    c.w("dve", "ou" + str(t % 2), 16 * ((t - 2) // 2 + 1))
                for j in range(4):
                    dcj = g * 4 + j
                    c.dve(lambda e, t=t, dcj=dcj, j=j, g=g: e.tensor_copy(
                        outstage[:, (t % 2) * D + (g * 4 + j) * 128:
                                 (t % 2) * D + (g * 4 + j) * 128 + 128],
                        pT[:, j * 128: j * 128 + 128]))
            out_dve_mark.append(c.now("dve"))
            c.w("sp", "dve", out_dve_mark[t])
            c.dma_sp(dy.ap()[t * 128:(t + 1) * 128, :],
                     outstage[:, (t % 2) * D:(t % 2) * D + D], "ou" + str(t % 2))
        c.w("sp", "ou0", c.now("ou0"))
        c.w("sp", "ou1", c.now("ou1"))

    with nc.Block() as block:
        @block.tensor
        def _(eng):
            program(Ctx("pe", eng, sems))

        @block.vector
        def _(eng):
            program(Ctx("dve", eng, sems))

        @block.scalar
        def _(eng):
            program(Ctx("act", eng, sems))

        @block.gpsimd
        def _(eng):
            program(Ctx("gp", eng, sems))

        @block.sync
        def _(eng):
            program(Ctx("sp", eng, sems))

    return nc


_NC_CACHE = {}


def _get_nc(S):
    if S not in _NC_CACHE:
        _NC_CACHE[S] = build_nc(S)
    return _NC_CACHE[S]


def make_inputs(b_x, Wqkv, bqkv, Wo, bo, wq, wk, W_r, W_k, W_v, mu_r, mu_k):
    return {
        "x": np.ascontiguousarray(b_x),
        "wqkvT": np.ascontiguousarray(Wqkv.T),
        "woT": np.ascontiguousarray(Wo.T),
        "wrT": np.ascontiguousarray(W_r.T),
        "wkT": np.ascontiguousarray(W_k.T),
        "wvT": np.ascontiguousarray(W_v.T),
        "bqkv": np.ascontiguousarray(bqkv.reshape(24, 128).T),
        "bo": np.ascontiguousarray(bo.reshape(8, 128).T),
        "mur": np.ascontiguousarray(mu_r.reshape(8, 128).T),
        "muk": np.ascontiguousarray(mu_k.reshape(8, 128).T),
        "wq2": np.ascontiguousarray(np.tile(wq, 2).reshape(128, 1)),
        "wk2": np.ascontiguousarray(np.tile(wk, 2).reshape(128, 1)),
    }


def kernel(L0_attn_x, L0_attn_Wqkv, L0_attn_bqkv, L0_attn_Wo, L0_attn_bo,
           L0_attn_wq, L0_attn_wk, L0_ffn_W_r, L0_ffn_W_k, L0_ffn_W_v,
           L0_ffn_mu_r, L0_ffn_mu_k):
    S = L0_attn_x.shape[1]
    nc = _get_nc(S)
    in_maps = []
    for b in range(NCORES):
        in_maps.append(make_inputs(
            L0_attn_x[b], L0_attn_Wqkv, L0_attn_bqkv, L0_attn_Wo, L0_attn_bo,
            L0_attn_wq, L0_attn_wk, L0_ffn_W_r, L0_ffn_W_k, L0_ffn_W_v,
            L0_ffn_mu_r, L0_ffn_mu_k))
    res = run_bass_kernel_spmd(nc, in_maps, core_ids=list(range(NCORES)))
    return np.stack([res.results[b]["y"] for b in range(NCORES)], axis=0)


# version 05
# speedup vs baseline: 1.3492x; 1.3467x over previous
import contextlib
import numpy as np
import concourse.bass as bass
import concourse.mybir as mybir
from concourse.bass_utils import run_bass_kernel_spmd

F32 = mybir.dt.float32
BF16 = mybir.dt.bfloat16
F32R = mybir.dt.float32r
AF = mybir.ActivationFunctionType
ALU = mybir.AluOpType
AX = mybir.AxisListType.X

D = 1024
DC = 8
H = 16
DH = 64
NCORES = 8


class Ctx:

    def __init__(self, active, eng, sems):
        self.active = active
        self.eng = eng
        self.h = sems
        self.cnt = {k: 0 for k in sems}

    def _fin(self, instr, inc, n=1):
        if inc is not None:
            self.cnt[inc] += n
            if instr is not None:
                instr.then_inc(self.h[inc], n)
        return instr

    def w(self, eng, sem, val):
        if self.active == eng and val > 0:
            self.eng.wait_ge(self.h[sem], val)

    def now(self, sem):
        return self.cnt[sem]

    def drain(self, eng):
        if self.active == eng and self.cnt[eng] > 0:
            self.eng.wait_ge(self.h[eng], self.cnt[eng])

    def pe(self, f):
        return self._fin(f(self.eng) if self.active == "pe" else None, "pe")

    def dve(self, f):
        return self._fin(f(self.eng) if self.active == "dve" else None, "dve")

    def act(self, f):
        return self._fin(f(self.eng) if self.active == "act" else None, "act")

    def gp(self, f):
        return self._fin(f(self.eng) if self.active == "gp" else None, "gp")

    def dma_sp(self, out, in_, sem):
        i = self.eng.dma_start(out=out, in_=in_) if self.active == "sp" else None
        return self._fin(i, sem, 16)

    def dma_gp(self, out, in_, sem):
        i = self.eng.dma_start(out=out, in_=in_) if self.active == "gp" else None
        return self._fin(i, sem, 16)


def build_nc(S=2048, upto=9):
    KC = S // 128
    QC = S // 512
    TT = S // 128
    nc = bass.Bass("TRN2", target_bir_lowering=False)

    dx = nc.dram_tensor("x", [S, D], F32, kind="ExternalInput")
    dwqkv = nc.dram_tensor("wqkvT", [D, 3 * D], F32, kind="ExternalInput")
    dwo = nc.dram_tensor("woT", [D, D], F32, kind="ExternalInput")
    dwr = nc.dram_tensor("wrT", [D, D], F32, kind="ExternalInput")
    dwk = nc.dram_tensor("wkT", [D, D], F32, kind="ExternalInput")
    dwv = nc.dram_tensor("wvT", [D, D], F32, kind="ExternalInput")
    dbqkv = nc.dram_tensor("bqkv", [128, 24], F32, kind="ExternalInput")
    dbo = nc.dram_tensor("bo", [128, 8], F32, kind="ExternalInput")
    dmur = nc.dram_tensor("mur", [128, 8], F32, kind="ExternalInput")
    dmuk = nc.dram_tensor("muk", [128, 8], F32, kind="ExternalInput")
    dwq2 = nc.dram_tensor("wq2", [128, 1], F32, kind="ExternalInput")
    dwk2 = nc.dram_tensor("wk2", [128, 1], F32, kind="ExternalInput")
    dy = nc.dram_tensor("y", [S, D], F32, kind="ExternalOutput")

    BASE = (nc.sbuf_base + 31) // 32 * 32
    off = [BASE]

    def alloc(name, shape, dtype, at=None):
        per = int(np.prod(shape[1:])) * mybir.dt.size(dtype)
        o = off[0] if at is None else at
        t = nc.alloc_sbuf_tensor_at(name, shape, dtype, offset=o)
        if at is None:
            off[0] = o + (per + 31) // 32 * 32
        return t

    ES = S
    x_sb = alloc("x_sb", [128, DC * S], F32)
    o_qt = off[0]
    qt = alloc("qt", [128, DC * S], BF16)
    xk = alloc("xk", [128, DC * S], BF16, at=o_qt)
    o_kt = off[0]
    kt = alloc("kt", [128, DC * S], BF16)
    rsig = alloc("rsig", [128, DC * S], BF16, at=o_kt)
    o_vt = off[0]
    vt = alloc("vt", [128, KC * H * (DH + 1)], BF16)
    xhat2 = alloc("xhat2", [128, DC * S], BF16, at=o_vt)
    kact = alloc("kact", [128, DC * S], BF16, at=o_vt)
    o_E = off[0]
    E_BYTES = max(2 * DC * S, 4096 + DC * 512 * 2 + 4 * DC * 128 * 2, 2 * D * 4)
    xhat = alloc("xhat", [128, DC * S], BF16, at=o_E)
    off[0] = o_E + E_BYTES
    instage = alloc("instage", [128, 2 * D], F32, at=o_E)
    outstage = alloc("outstage", [128, 2 * D], F32, at=o_E)
    xr = alloc("xr", [128, DC * S], BF16, at=o_E)
    attn = alloc("attn", [128, 2 * 2 * 512], BF16, at=o_E)
    zs = alloc("zs", [128, DC * 512], BF16, at=o_E + 4096)
    wos = alloc("wos", [128, 4 * DC * 128], BF16, at=o_E + 4096 + 8192)
    wst = alloc("wst", [128, 2 * DC * 128], BF16)
    tq = alloc("tq", [128, 512], F32)
    sqb = alloc("sqb", [128, 512], BF16)
    xbt = alloc("xbt", [128, 512], BF16)
    xqt = alloc("xqt", [128, 512], BF16)
    stats = alloc("stats", [128, 512], F32)
    statr = alloc("statr", [128, 512], F32R)
    small = alloc("small", [128, 32], F32)
    aones_b = alloc("aones_b", [128, 1], BF16)
    honesA_b = alloc("honesA_b", [128, 1], BF16)
    honesB_b = alloc("honesB_b", [128, 1], BF16)
    bqkv_sb = alloc("bqkv_sb", [128, 24], F32)
    bo_sb = alloc("bo_sb", [128, 8], F32)
    mur_sb = alloc("mur_sb", [128, 8], F32)
    muk_sb = alloc("muk_sb", [128, 8], F32)
    wq2_sb = alloc("wq2_sb", [128, 1], F32)
    wk2_sb = alloc("wk2_sb", [128, 1], F32)
    identf = alloc("identf", [128, 128], F32)
    rowsr = alloc("rowsr", [128, 256], F32R)
    nb = alloc("nb", [128, 8], F32)
    nbr = alloc("nbr", [128, 8], F32R)
    norm_acc = alloc("norm_acc", [128, 72], F32)
    csts = alloc("csts", [128, 8], F32)
    assert off[0] <= 229344, f"sbuf overflow: {off[0]}"
    nc.alloc_sbuf_tensor("reserve_manual", [128, off[0] - BASE], mybir.dt.uint8)
    rowsf = tq
    lrec = stats
    lrec_r = statr
    rz_sb = tq
    sqf = tq

    pA = nc.alloc_psum_tensor("pA", [128, 1024], F32)
    pB = nc.alloc_psum_tensor("pB", [128, 1024], F32)
    pZ = nc.alloc_psum_tensor("pZ", [128, 1024], F32)
    pM = nc.alloc_psum_tensor("pM", [128, 512], F32)
    pT = nc.alloc_psum_tensor("pT", [128, 512], F32)

    nc.const_aps.register(F32, 1e-5, csts[:, 0:1])
    nc.const_aps.register(F32, 1e-6, csts[:, 1:2])
    nc.const_aps.register(F32, 1.0 / DH, csts[:, 2:3])
    nc.const_aps.register(F32, -0.5, csts[:, 5:6])
    nc.const_aps.register(F32, -1.0, csts[:, 6:7])

    sem_names = ["pe", "dve", "act", "gp", "ldp", "lq0", "lq1",
                 "wo0", "wo1", "wo2", "wo3", "ldh0", "ldh1", "ou0", "ou1"]
    cm = contextlib.ExitStack()
    sems = {n: cm.enter_context(nc.semaphore("s_" + n)) for n in sem_names}

    def xs(dc, a, b):
        return x_sb[:, dc * ES + a: dc * ES + b]

    def program(c: Ctx):
        c.gp(lambda e: e.iota(identf[:, :], [[-1, 128]], channel_multiplier=1,
                              allow_small_or_imprecise_dtypes=True))
        c.gp(lambda e: e.memset(rowsf[:, :], 0.0))
        c.drain("gp")
        for _r in (0, 32, 64):
            c.gp(lambda e, _r=_r: e.memset(rowsf[_r:_r + 1, 0:128], 1.0))
            c.drain("gp")
        for _r in (0, 64):
            c.gp(lambda e, _r=_r: e.memset(rowsf[_r:_r + 1, 128:192], 1.0))
            c.drain("gp")
        c.gp(lambda e: e.memset(rowsf[32:33, 192:256], 1.0))
        c.drain("gp")
        c.gp(lambda e: e.memset(aones_b[:, :], 1.0))
        c.drain("gp")
        c.gp(lambda e: e.memset(honesA_b[:, :], 0.0))
        c.drain("gp")
        c.gp(lambda e: e.memset(honesA_b[0:64, :], 1.0))
        c.drain("gp")
        c.gp(lambda e: e.memset(honesB_b[:, :], 0.0))
        c.drain("gp")
        c.gp(lambda e: e.memset(honesB_b[64:128, :], 1.0))
        c.drain("gp")
        c.gp(lambda e: e.memset(vt[:, :], 1.0))
        c.drain("gp")
        c.gp(lambda e: e.memset(csts[:, 0:1], 1e-5))
        c.drain("gp")
        c.gp(lambda e: e.memset(csts[:, 1:2], 1e-6))
        c.drain("gp")
        c.gp(lambda e: e.memset(csts[:, 2:3], 1.0 / DH))
        c.drain("gp")
        c.gp(lambda e: e.memset(csts[:, 3:4], 0.0))
        c.drain("gp")
        c.gp(lambda e: e.memset(csts[:, 4:5], 1.0))
        c.drain("gp")
        c.gp(lambda e: e.memset(csts[:, 5:6], -0.5))
        c.drain("gp")
        c.gp(lambda e: e.memset(csts[:, 6:7], -1.0))
        c.drain("gp")
        gp_const = c.now("gp")

        c.dma_gp(bqkv_sb[:, :], dbqkv.ap(), "ldp")
        c.dma_gp(bo_sb[:, :], dbo.ap(), "ldp")
        c.dma_gp(mur_sb[:, :], dmur.ap(), "ldp")
        c.dma_gp(muk_sb[:, :], dmuk.ap(), "ldp")
        c.dma_gp(wq2_sb[:, :], dwq2.ap(), "ldp")
        c.dma_gp(wk2_sb[:, :], dwk2.ap(), "ldp")
        lds_params = c.now("ldp")

        c.w("dve", "gp", gp_const)
        c.dve(lambda e: e.tensor_single_scalar(identf[:, :], identf[:, :], 0.0, ALU.is_equal))
        c.dve(lambda e: e.tensor_copy(rowsr[:, :], rowsf[:, 0:256]))
        c.w("dve", "ldp", lds_params)
        c.dve(lambda e: e.tensor_mul(small[:, 0:1], wq2_sb[:, :], wk2_sb[:, :]))
        dve_const = c.now("dve")

        dve_tile_mark = []
        for t in range(TT):
            if t >= 2:
                c.w("sp", "dve", dve_tile_mark[t - 2])
            hsem = "ldh" + str(t % 2)
            c.dma_sp(instage[:, (t % 2) * D:(t % 2) * D + D],
                     dx.ap()[t * 128:(t + 1) * 128, :], hsem)
            c.w("pe", hsem, c.now(hsem))
            if t == 0:
                c.w("pe", "dve", dve_const)
            for g in range(2):
                if t > 0 or g > 0:
                    c.w("pe", "dve", c.now("dve"))
                for j in range(4):
                    dcj = g * 4 + j
                    c.pe(lambda e, t=t, dcj=dcj, j=j: e.transpose(
                        pT[:, j * 128: j * 128 + 128],
                        instage[:, (t % 2) * D + dcj * 128:(t % 2) * D + dcj * 128 + 128],
                        identf[:, :]))
                pe_here = c.now("pe")
                c.w("dve", "pe", pe_here)
                for j in range(4):
                    dcj = g * 4 + j
                    c.dve(lambda e, t=t, dcj=dcj, j=j: e.tensor_copy(
                        xs(dcj, t * 128, t * 128 + 128),
                        pT[:, j * 128: j * 128 + 128]))
            dve_tile_mark.append(c.now("dve"))
        x_ready = c.now("dve")

        if upto < 1:
            return finish(c)
        def layernorm(dst, qc):
            q0 = qc * 512
            for dc in range(DC):
                c.w("dve", "pe", c.now("pe"))
                c.dve(lambda e, dc=dc, q0=q0: e.tensor_copy(
                    xbt[:, :], xs(dc, q0, q0 + 512)))
                c.dve(lambda e, dc=dc, q0=q0: e.tensor_mul(
                    xqt[:, :], xs(dc, q0, q0 + 512), xs(dc, q0, q0 + 512)))
                d_here = c.now("dve")
                c.w("pe", "dve", d_here)
                c.pe(lambda e, dc=dc: e.matmul(
                    pB[0:1, 0:512], aones_b[:, :], xbt[:, :],
                    start=(dc == 0), stop=(dc == DC - 1), tile_position=(0, 0)))
                c.pe(lambda e, dc=dc: e.matmul(
                    pB[32:33, 512:1024], aones_b[:, :], xqt[:, :],
                    start=(dc == 0), stop=(dc == DC - 1), tile_position=(0, 32)))
            pe_stats = c.now("pe")
            c.w("dve", "pe", pe_stats)
            c.dve(lambda e: e.tensor_copy(statr[0:1, :], pB[0:1, 0:512]))
            c.dve(lambda e: e.tensor_copy(statr[32:33, :], pB[32:33, 512:1024]))
            dve_sr = c.now("dve")
            c.w("pe", "dve", dve_sr)
            c.pe(lambda e: e.matmul(pM[:, 0:512], rowsr[0:1, 0:128], statr[0:1, :],
                                    start=True, stop=True))
            c.pe(lambda e: e.matmul(pB[:, 0:512], rowsr[32:33, 0:128], statr[32:33, :],
                                    start=True, stop=True))
            pe_bc = c.now("pe")
            c.w("dve", "pe", pe_bc)
            c.drain("dve")
            c.dve(lambda e: e.tensor_scalar(tq[:, 0:512], pM[:, 0:512],
                                            1.0 / D, None, ALU.mult))
            c.drain("dve")
            c.dve(lambda e: e.tensor_mul(stats[:, :], tq[:, 0:512], tq[:, 0:512]))
            c.drain("dve")
            c.dve(lambda e: e.scalar_tensor_tensor(stats[:, :], pB[:, 0:512],
                                                   1.0 / D, stats[:, :],
                                                   ALU.mult, ALU.subtract))
            dve_v = c.now("dve")
            c.w("act", "dve", dve_v)
            c.act(lambda e: e.activation(pM[:, 0:512], stats[:, :], AF.Sqrt,
                                         bias=1e-5, scale=1.0))
            act_sd = c.now("act")
            c.w("dve", "act", act_sd)
            c.dve(lambda e: e.reciprocal(stats[:, :], pM[:, 0:512]))
            c.drain("dve")
            c.dve(lambda e: e.scalar_tensor_tensor(pM[:, 0:512], tq[:, 0:512], -1.0,
                                                   stats[:, :], ALU.mult, ALU.mult))
            c.drain("dve")
            dve_ab = c.now("dve")
            for dc in range(DC):
                c.dve(lambda e, dc=dc, q0=q0: e.tensor_mul(
                    tq[:, 0:512], xs(dc, q0, q0 + 512), stats[:, :]))
                c.drain("dve")
                c.dve(lambda e, dc=dc, q0=q0: e.tensor_add(
                    dst[:, dc * ES + q0: dc * ES + q0 + 512], tq[:, 0:512],
                    pM[:, 0:512]))
                c.drain("dve")
            return c.now("pe"), c.now("dve")

        c.w("pe", "dve", x_ready)
        for qc in range(QC):
            layernorm(xhat, qc)
        ln1_pe, ln1_dve = c.now("pe"), c.now("dve")

        if upto < 2:
            return finish(c)
        def wqkv_ap(oc):
            return bass.AP(dwqkv, oc * 128, [[3 * D, 128], [128 * 3 * D, DC], [1, 128]])

        qkv_pe_mark = [0] * 24
        lds_mark = [0] * 24
        NCH = 24 * QC
        mm_mark = [0] * NCH
        evac_mark = [(0, 0, 0)] * NCH
        eva_act = [0] * NCH

        def emit_chain_mms(i):
            oc, qc = divmod(i, QC)
            if qc == 0:
                if oc == 0:
                    c.dma_gp(wst[:, 0:DC * 128], wqkv_ap(0), "lq0")
                    lds_mark[0] = c.now("lq0")
                    c.dma_gp(wst[:, DC * 128:2 * DC * 128], wqkv_ap(1), "lq1")
                    lds_mark[1] = c.now("lq1")
                elif oc + 1 < 24:
                    c.w("gp", "pe", qkv_pe_mark[oc - 1])
                    sl = ((oc + 1) % 2) * DC * 128
                    c.dma_gp(wst[:, sl:sl + DC * 128], wqkv_ap(oc + 1), "lq" + str((oc + 1) % 2))
                    lds_mark[oc + 1] = c.now("lq" + str((oc + 1) % 2))
                c.w("pe", "lq" + str(oc % 2), lds_mark[oc])
            if i == 0:
                c.w("pe", "dve", ln1_dve)
            bank = (i % 2) * 512
            if i >= 2:
                c.w("pe", "act", eva_act[i - 2])
            wsl = (oc % 2) * DC * 128
            q0 = qc * 512
            for dc in range(DC):
                c.pe(lambda e, dc=dc, q0=q0, bank=bank, wsl=wsl: e.matmul(
                    pA[:, bank:bank + 512],
                    wst[:, wsl + dc * 128: wsl + dc * 128 + 128],
                    xhat[:, dc * ES + q0: dc * ES + q0 + 512],
                    start=(dc == 0), stop=(dc == DC - 1)))
            mm_mark[i] = c.now("pe")
            if qc == QC - 1:
                qkv_pe_mark[oc] = c.now("pe")

        def emit_tail(i):
            oc, qc = divmod(i, QC)
            q0 = qc * 512
            bank = (i % 2) * 512
            pair = oc % 8
            c.w("act", "pe", mm_mark[i])
            if i >= 1:
                c.w("act", "dve", evac_mark[i - 1][1])
                c.w("act", "pe", evac_mark[i - 1][2])
            c.act(lambda e, oc=oc, bank=bank: e.activation(
                tq[:, 0:512], pA[:, bank:bank + 512], AF.Identity,
                bias=bqkv_sb[:, oc:oc + 1], scale=1.0))
            eva_act[i] = c.now("act")
            if oc < 16:
                c.drain("act")
                c.act(lambda e: e.activation(sqb[:, :], tq[:, 0:512], AF.Square))
                act_sq = c.now("act")
                c.w("pe", "dve", 0)
                c.w("pe", "act", act_sq)
                c.pe(lambda e: e.matmul(pB[0:1, 0:512], honesA_b[:, :], sqb[:, :],
                                        start=True, stop=True, tile_position=(0, 0)))
                c.pe(lambda e: e.matmul(pB[32:33, 512:1024], honesB_b[:, :], sqb[:, :],
                                        start=True, stop=True, tile_position=(0, 32)))
                pe_rms = c.now("pe")
                c.w("dve", "pe", pe_rms)
                c.drain("dve")
                c.dve(lambda e: e.tensor_copy(statr[0:1, :], pB[0:1, 0:512]))
                c.dve(lambda e: e.tensor_copy(statr[32:33, :], pB[32:33, 512:1024]))
                dve_sr2 = c.now("dve")
                c.w("pe", "dve", dve_sr2)
                c.pe(lambda e: e.matmul(pB[:, 0:512], rowsr[0:1, 0:128], statr[0:1, :],
                                        start=True, stop=True))
                c.pe(lambda e: e.matmul(pB[:, 512:1024], rowsr[32:33, 0:128], statr[32:33, :],
                                        start=True, stop=True))
                pe_ssq = c.now("pe")
                c.w("act", "pe", pe_ssq)
                c.act(lambda e: e.activation(pM[0:64, 0:512], pB[0:64, 0:512], AF.Sqrt,
                                             bias=1e-6, scale=1.0 / DH))
                c.act(lambda e: e.activation(pM[64:128, 0:512], pB[64:128, 512:1024], AF.Sqrt,
                                             bias=csts[64:128, 1:2], scale=csts[64:128, 2:3]))
                act_sd2 = c.now("act")
                c.w("dve", "act", act_sd2)
                c.drain("dve")
                c.dve(lambda e: e.reciprocal(stats[:, :], pM[:, 0:512]))
                c.drain("dve")
                if oc < 8:
                    c.dve(lambda e, pair=pair, q0=q0: e.scalar_tensor_tensor(
                        qt[:, pair * ES + q0: pair * ES + q0 + 512],
                        tq[:, 0:512], small[:, 0:1], stats[:, :],
                        ALU.mult, ALU.mult))
                else:
                    c.dve(lambda e, pair=pair, q0=q0: e.tensor_mul(
                        kt[:, pair * ES + q0: pair * ES + q0 + 512],
                        tq[:, 0:512], stats[:, :]))
                dve_apply = c.now("dve")
                dstt = qt if oc < 8 else kt
                c.w("act", "dve", dve_apply)
                c.act(lambda e, pair=pair, q0=q0, dstt=dstt: e.activation(
                    sqb[:, :], dstt[:, pair * ES + q0: pair * ES + q0 + 512], AF.Square))
                act_csq = c.now("act")
                c.w("pe", "act", act_csq)
                c.pe(lambda e: e.matmul(pM[0:1, 0:512], honesA_b[:, :], sqb[:, :],
                                        start=True, stop=True, tile_position=(0, 0)))
                c.pe(lambda e: e.matmul(pM[32:33, 0:512], honesB_b[:, :], sqb[:, :],
                                        start=True, stop=True, tile_position=(0, 32)))
                pe_cs = c.now("pe")
                col = (0 if oc < 8 else 36) + pair * QC + qc
                c.w("dve", "pe", pe_cs)
                c.dve(lambda e, col=col: e.reduce_max(
                    norm_acc[0:1, col:col + 1], pM[0:1, 0:512], axis=AX))
                c.dve(lambda e, col=col: e.reduce_max(
                    norm_acc[32:33, col:col + 1], pM[32:33, 0:512], axis=AX))
            else:
                vc = oc - 16
                c.w("pe", "act", eva_act[i])
                if i >= 1:
                    c.w("pe", "dve", c.now("dve"))
                for tb in range(4):
                    c.pe(lambda e, tb=tb: e.transpose(
                        pT[:, tb * 128:tb * 128 + 128],
                        tq[:, tb * 128: tb * 128 + 128], identf[:, :]))
                pe_vt = c.now("pe")
                c.w("dve", "pe", pe_vt)
                for tb in range(4):
                    kc = qc * 4 + tb
                    base = kc * H * (DH + 1) + vc * 2 * (DH + 1)
                    c.dve(lambda e, tb=tb, base=base: e.tensor_copy(
                        bass.AP(vt, base, [[KC * H * (DH + 1), 128], [DH + 1, 2], [1, DH]]),
                        pT[:, tb * 128:tb * 128 + 128].rearrange("p (g n) -> p g n", g=2)))
            evac_mark[i] = (c.now("act"), c.now("dve"), c.now("pe"))

        for i in range(NCH + 1):
            if i < NCH:
                emit_chain_mms(i)
            if i >= 1:
                emit_tail(i - 1)
        qkv_pe = c.now("pe")

        if upto < 3:
            return finish(c)
        c.w("dve", "pe", qkv_pe)
        c.drain("dve")
        for pair in range(8):
            qa, ka = pair * QC, 36 + pair * QC
            for r in (0, 32):
                c.dve(lambda e, qa=qa, r=r: e.reduce_max(
                    small[r:r + 1, 2:3], norm_acc[r:r + 1, qa:qa + QC], axis=AX))
                c.dve(lambda e, ka=ka, r=r: e.reduce_max(
                    small[r:r + 1, 3:4], norm_acc[r:r + 1, ka:ka + QC], axis=AX))
                c.drain("dve")
                c.dve(lambda e, r=r: e.tensor_mul(small[r:r + 1, 4:5],
                                                  small[r:r + 1, 2:3], small[r:r + 1, 3:4]))
                c.drain("dve")
            dve_cs2 = c.now("dve")
            c.w("act", "dve", dve_cs2)
            c.act(lambda e: e.activation(small[0:1, 5:6], small[0:1, 4:5], AF.Sqrt))
            c.act(lambda e: e.activation(small[32:33, 5:6], small[32:33, 4:5], AF.Sqrt,
                                         bias=csts[32:33, 3:4], scale=csts[32:33, 4:5]))
            act_cs = c.now("act")
            c.w("dve", "act", act_cs)
            for r in (0, 32):
                c.dve(lambda e, pair=pair, r=r: e.tensor_scalar(
                    nb[r:r + 1, pair:pair + 1], small[r:r + 1, 5:6],
                    -1.0, 71.0, ALU.mult, ALU.add))
                c.drain("dve")
                c.dve(lambda e, pair=pair, r=r: e.tensor_copy(
                    nbr[r:r + 1, pair:pair + 1], nb[r:r + 1, pair:pair + 1]))
                c.drain("dve")
        dve_nb = c.now("dve")
        c.w("pe", "dve", dve_nb)
        c.pe(lambda e: e.matmul(pM[:, 0:8], rowsr[0:1, 0:128],
                                nbr[0:1, 0:8], start=True, stop=True))
        c.pe(lambda e: e.matmul(pM[:, 8:16], rowsr[32:33, 0:128],
                                nbr[32:33, 0:8], start=True, stop=True))
        pe_nb = c.now("pe")
        c.w("dve", "pe", pe_nb)
        for pair in range(8):
            for hh in range(2):
                c.dve(lambda e, pair=pair, hh=hh: e.tensor_copy(
                    small[:, 16 + 2 * pair + hh:17 + 2 * pair + hh],
                    pM[:, hh * 8 + pair:hh * 8 + pair + 1]))
        dve_bias = c.now("dve")

        if upto < 4:
            return finish(c)
        last_exp = {0: 0, 1: 0}
        last_z = {0: 0, 1: 0}
        wo_dve_mark = 0
        wo_pe_mark = [0] * 4
        lds_wo = [0] * (QC * 8)
        first_att = True

        zevac_dve = 0
        for qc in range(QC):
            q0 = qc * 512
            for pair in range(8):
                if first_att:
                    c.w("pe", "dve", dve_bias)
                    first_att = False
                if pair == 0 and qc > 0:
                    c.w("pe", "dve", wo_dve_mark)
                if pair < 4:
                    toc = pair
                    slot = toc % 4
                    if qc == 0:
                        c.w("gp", "pe", qkv_pe)
                    c.w("gp", "pe", wo_pe_mark[slot])
                    c.dma_gp(wos[:, slot * DC * 128:(slot + 1) * DC * 128],
                             bass.AP(dwo, toc * 128, [[D, 128], [128 * D, DC], [1, 128]]),
                             "wo" + str(slot))
                    lds_wo[qc * 8 + toc] = c.now("wo" + str(slot))

                zm = {}
                exp_mark = {}
                zslot_pe = dict(last_z)
                for kc in range(KC + 1):
                    if kc < KC:
                        sl = kc % 2
                        c.w("pe", "act", last_exp[sl])
                        c.pe(lambda e, pair=pair, kc=kc, sl=sl, q0=q0: e.matmul(
                            pA[:, sl * 512: sl * 512 + 512],
                            kt[0:64, pair * ES + kc * 128: pair * ES + kc * 128 + 128],
                            qt[0:64, pair * ES + q0: pair * ES + q0 + 512],
                            start=True, stop=True, tile_position=(0, 0)))
                        c.pe(lambda e, pair=pair, kc=kc, sl=sl, q0=q0: e.matmul(
                            pB[:, sl * 512: sl * 512 + 512],
                            kt[64:128, pair * ES + kc * 128: pair * ES + kc * 128 + 128],
                            qt[64:128, pair * ES + q0: pair * ES + q0 + 512],
                            start=True, stop=True, tile_position=(64, 0)))
                        zm[kc] = c.now("pe")
                    if kc >= 1:
                        kz = kc - 1
                        slz = kz % 2
                        c.w("pe", "act", exp_mark[kz])
                        if kz == 0:
                            c.w("pe", "dve", zevac_dve)
                        for hh in range(2):
                            hl = 2 * pair + hh
                            c.pe(lambda e, hh=hh, hl=hl, kz=kz, slz=slz: e.matmul(
                                pZ[0:65, hh * 512: hh * 512 + 512],
                                vt[:, kz * H * (DH + 1) + hl * (DH + 1):
                                   kz * H * (DH + 1) + hl * (DH + 1) + DH + 1],
                                attn[:, (hh * 2 + slz) * 512:(hh * 2 + slz) * 512 + 512],
                                start=(kz == 0), stop=(kz == KC - 1)))
                        last_z[slz] = c.now("pe")
                    if kc < KC:
                        sl = kc % 2
                        c.w("act", "pe", max(zm[kc], last_z[sl] if kc >= 2 else zslot_pe[sl]))
                        c.act(lambda e, pair=pair, sl=sl: e.activation(
                            attn[:, (0 * 2 + sl) * 512:(0 * 2 + sl) * 512 + 512],
                            pA[:, sl * 512: sl * 512 + 512], AF.Exp,
                            bias=small[:, 16 + 2 * pair:17 + 2 * pair], scale=1.0))
                        c.act(lambda e, pair=pair, sl=sl: e.activation(
                            attn[:, (1 * 2 + sl) * 512:(1 * 2 + sl) * 512 + 512],
                            pB[:, sl * 512: sl * 512 + 512], AF.Exp,
                            bias=small[:, 17 + 2 * pair:18 + 2 * pair], scale=1.0))
                        last_exp[sl] = c.now("act")
                        exp_mark[kc] = last_exp[sl]
                z_pe = c.now("pe")
                for hh in range(2):
                    c.w("dve", "pe", z_pe)
                    c.dve(lambda e, hh=hh: e.reciprocal(lrec[0:1, 0:512],
                                                        pZ[64:65, hh * 512: hh * 512 + 512]))
                    c.drain("dve")
                    c.dve(lambda e: e.tensor_copy(statr[0:1, 0:512], lrec[0:1, 0:512]))
                    dve_lr = c.now("dve")
                    c.w("pe", "dve", dve_lr)
                    c.pe(lambda e: e.matmul(pM[0:64, 0:512], rowsr[0:1, 128:192],
                                            statr[0:1, 0:512], start=True, stop=True))
                    pe_rz = c.now("pe")
                    c.w("dve", "pe", pe_rz)
                    c.dve(lambda e: e.tensor_copy(rz_sb[0:64, 0:512], pM[0:64, 0:512]))
                    c.drain("dve")
                    r0 = hh * 64
                    c.dve(lambda e, hh=hh, pair=pair, r0=r0: e.tensor_mul(
                        zs[r0:r0 + 64, pair * 512:(pair + 1) * 512],
                        pZ[0:64, hh * 512: hh * 512 + 512], rz_sb[0:64, 0:512]))
                zevac_dve = c.now("dve")
            zs_dve = c.now("dve")
            att_act = c.now("act")
            wo_evac = 0
            for oc in range(8):
                bank = (oc % 2) * 512
                c.w("pe", "wo" + str(oc % 4), lds_wo[qc * 8 + oc])
                if oc == 0:
                    c.w("pe", "dve", zs_dve)
                    c.w("pe", "act", att_act)
                if oc >= 2:
                    c.w("pe", "dve", c.now("dve"))
                slot = oc % 4
                for dc in range(DC):
                    c.pe(lambda e, oc=oc, dc=dc, bank=bank, slot=slot: e.matmul(
                        pA[:, bank:bank + 512],
                        wos[:, slot * DC * 128 + dc * 128: slot * DC * 128 + dc * 128 + 128],
                        zs[:, dc * 512:(dc + 1) * 512],
                        start=(dc == 0), stop=(dc == DC - 1)))
                wo_pe_mark[slot] = c.now("pe")
                pe_wo = c.now("pe")
                if oc + 4 < 8:
                    toc = oc + 4
                    c.w("gp", "pe", wo_pe_mark[slot])
                    c.dma_gp(wos[:, slot * DC * 128:(slot + 1) * DC * 128],
                             bass.AP(dwo, toc * 128, [[D, 128], [128 * D, DC], [1, 128]]),
                             "wo" + str(slot))
                    lds_wo[qc * 8 + toc] = c.now("wo" + str(slot))
                c.w("dve", "pe", pe_wo)
                c.dve(lambda e, oc=oc, q0=q0, bank=bank: e.scalar_tensor_tensor(
                    xs(oc, q0, q0 + 512), pA[:, bank:bank + 512], bo_sb[:, oc:oc + 1],
                    xs(oc, q0, q0 + 512), ALU.add, ALU.add))
            wo_dve_mark = c.now("dve")
        att_done_pe = c.now("pe")
        att_done_dve = c.now("dve")

        if upto < 5:
            return finish(c)
        c.w("pe", "dve", att_done_dve)
        for qc in range(QC):
            layernorm(xhat2, qc)
        ln2_dve = c.now("dve")

        for dc in range(DC):
            for qc in range(QC):
                q0 = qc * 512
                a = dc * ES + q0
                w0 = 1 if qc == 0 else 0
                c.drain("dve")
                c.dve(lambda e, a=a, w0=w0: e.tensor_sub(
                    pM[:, w0:512], xhat2[:, a + w0: a + 512], xhat2[:, a + w0 - 1: a + 511]))
                c.drain("dve")
                c.dve(lambda e, a=a, w0=w0, dc=dc: e.scalar_tensor_tensor(
                    xr[:, a + w0: a + 512], pM[:, w0:512], mur_sb[:, dc:dc + 1],
                    xhat2[:, a + w0 - 1: a + 511], ALU.mult, ALU.add))
                c.dve(lambda e, a=a, w0=w0, dc=dc: e.scalar_tensor_tensor(
                    xk[:, a + w0: a + 512], pM[:, w0:512], muk_sb[:, dc:dc + 1],
                    xhat2[:, a + w0 - 1: a + 511], ALU.mult, ALU.add))
                if qc == 0:
                    c.dve(lambda e, a=a, dc=dc: e.tensor_scalar_mul(
                        xr[:, a:a + 1], xhat2[:, a:a + 1], mur_sb[:, dc:dc + 1]))
                    c.dve(lambda e, a=a, dc=dc: e.tensor_scalar_mul(
                        xk[:, a:a + 1], xhat2[:, a:a + 1], muk_sb[:, dc:dc + 1]))
        blend_dve = c.now("dve")

        if upto < 6:
            return finish(c)
        def wffn_ap(dw, oc):
            return bass.AP(dw, oc * 128, [[D, 128], [128 * D, DC], [1, 128]])

        ffn_pe_mark = [0, 0]
        ffn_lds = [0] * 24
        fi = 0
        for phase, (dw, src, kind) in enumerate([
                (dwr, xr, "r"), (dwk, xk, "k"), (dwv, kact, "v")]):
            for oc in range(8):
                gi = phase * 8 + oc
                if gi == 0:
                    c.w("gp", "pe", qkv_pe_mark[23])
                    c.dma_gp(wst[:, 0:DC * 128], wffn_ap(dwr, 0), "lq0")
                    ffn_lds[0] = c.now("lq0")
                    c.dma_gp(wst[:, DC * 128:], wffn_ap(dwr, 1), "lq1")
                    ffn_lds[1] = c.now("lq1")
                elif gi + 1 < 24:
                    nphase, noc = divmod(gi + 1, 8)
                    ndw = [dwr, dwk, dwv][nphase]
                    c.w("gp", "pe", ffn_pe_mark[(gi - 1) % 2])
                    sl = ((gi + 1) % 2) * DC * 128
                    c.dma_gp(wst[:, sl:sl + DC * 128], wffn_ap(ndw, noc), "lq" + str((gi + 1) % 2))
                    ffn_lds[gi + 1] = c.now("lq" + str((gi + 1) % 2))
                c.w("pe", "lq" + str(gi % 2), ffn_lds[gi])
                if gi == 0:
                    c.w("pe", "dve", blend_dve)
                wsl = (gi % 2) * DC * 128
                for qc in range(QC):
                    q0 = qc * 512
                    bank = (fi % 2) * 512
                    if fi >= 2:
                        am, dm, _pm = evac_mark[fi - 2]
                        c.w("pe", "act", am)
                        c.w("pe", "dve", dm)
                    if kind == "v" and fi == 2 * 8 * QC:
                        c.w("pe", "dve", evac_mark[2 * 8 * QC - 1][1])
                    for dc in range(DC):
                        c.pe(lambda e, dc=dc, q0=q0, bank=bank, wsl=wsl, src=src: e.matmul(
                            pA[:, bank:bank + 512],
                            wst[:, wsl + dc * 128: wsl + dc * 128 + 128],
                            src[:, dc * ES + q0: dc * ES + q0 + 512],
                            start=(dc == 0), stop=(dc == DC - 1)))
                    pe_mm = c.now("pe")
                    if kind == "r":
                        c.w("act", "pe", pe_mm)
                        c.act(lambda e, oc=oc, q0=q0, bank=bank: e.activation(
                            rsig[:, oc * ES + q0: oc * ES + q0 + 512],
                            pA[:, bank:bank + 512], AF.Sigmoid))
                    elif kind == "k":
                        c.w("dve", "pe", pe_mm)
                        c.drain("dve")
                        c.dve(lambda e, bank=bank: e.tensor_scalar_max(
                            sqf[:, 0:512], pA[:, bank:bank + 512], 0.0))
                        c.drain("dve")
                        c.dve(lambda e, oc=oc, q0=q0: e.tensor_mul(
                            kact[:, oc * ES + q0: oc * ES + q0 + 512], sqf[:, 0:512], sqf[:, 0:512]))
                    else:
                        c.w("dve", "pe", pe_mm)
                        c.drain("dve")
                        c.dve(lambda e, oc=oc, q0=q0, bank=bank: e.tensor_mul(
                            sqf[:, 0:512], pA[:, bank:bank + 512],
                            rsig[:, oc * ES + q0: oc * ES + q0 + 512]))
                        c.drain("dve")
                        c.dve(lambda e, oc=oc, q0=q0: e.tensor_add(
                            xs(oc, q0, q0 + 512), xs(oc, q0, q0 + 512), sqf[:, 0:512]))
                    evac_mark[fi] = (c.now("act"), c.now("dve"), c.now("pe"))
                    fi += 1
                ffn_pe_mark[gi % 2] = c.now("pe")
        ffn_dve = c.now("dve")
        ffn_act = c.now("act")

        return finish(c)

    def finish(c):
        c.w("pe", "dve", c.now("dve"))
        c.w("pe", "act", c.now("act"))
        c.w("pe", "gp", c.now("gp"))
        out_dve_mark = []
        for t in range(TT):
            for g in range(2):
                c.w("pe", "dve", c.now("dve"))
                for j in range(4):
                    dcj = g * 4 + j
                    c.pe(lambda e, t=t, dcj=dcj, j=j: e.transpose(
                        pT[:, j * 128: j * 128 + 128],
                        xs(dcj, t * 128, t * 128 + 128), identf[:, :]))
                pe_here = c.now("pe")
                c.w("dve", "pe", pe_here)
                if t >= 2 and g == 0:
                    c.w("dve", "ou" + str(t % 2), 16 * ((t - 2) // 2 + 1))
                for j in range(4):
                    dcj = g * 4 + j
                    c.dve(lambda e, t=t, dcj=dcj, j=j, g=g: e.tensor_copy(
                        outstage[:, (t % 2) * D + (g * 4 + j) * 128:
                                 (t % 2) * D + (g * 4 + j) * 128 + 128],
                        pT[:, j * 128: j * 128 + 128]))
            out_dve_mark.append(c.now("dve"))
            c.w("sp", "dve", out_dve_mark[t])
            c.dma_sp(dy.ap()[t * 128:(t + 1) * 128, :],
                     outstage[:, (t % 2) * D:(t % 2) * D + D], "ou" + str(t % 2))
        c.w("sp", "ou0", c.now("ou0"))
        c.w("sp", "ou1", c.now("ou1"))

    with nc.Block() as block:
        @block.tensor
        def _(eng):
            program(Ctx("pe", eng, sems))

        @block.vector
        def _(eng):
            program(Ctx("dve", eng, sems))

        @block.scalar
        def _(eng):
            program(Ctx("act", eng, sems))

        @block.gpsimd
        def _(eng):
            program(Ctx("gp", eng, sems))

        @block.sync
        def _(eng):
            program(Ctx("sp", eng, sems))

    return nc


_NC_CACHE = {}


def _get_nc(S):
    if S not in _NC_CACHE:
        _NC_CACHE[S] = build_nc(S)
    return _NC_CACHE[S]


def make_inputs(b_x, Wqkv, bqkv, Wo, bo, wq, wk, W_r, W_k, W_v, mu_r, mu_k):
    return {
        "x": np.ascontiguousarray(b_x),
        "wqkvT": np.ascontiguousarray(Wqkv.T),
        "woT": np.ascontiguousarray(Wo.T),
        "wrT": np.ascontiguousarray(W_r.T),
        "wkT": np.ascontiguousarray(W_k.T),
        "wvT": np.ascontiguousarray(W_v.T),
        "bqkv": np.ascontiguousarray(bqkv.reshape(24, 128).T),
        "bo": np.ascontiguousarray(bo.reshape(8, 128).T),
        "mur": np.ascontiguousarray(mu_r.reshape(8, 128).T),
        "muk": np.ascontiguousarray(mu_k.reshape(8, 128).T),
        "wq2": np.ascontiguousarray(np.tile(wq, 2).reshape(128, 1)),
        "wk2": np.ascontiguousarray(np.tile(wk, 2).reshape(128, 1)),
    }


def kernel(L0_attn_x, L0_attn_Wqkv, L0_attn_bqkv, L0_attn_Wo, L0_attn_bo,
           L0_attn_wq, L0_attn_wk, L0_ffn_W_r, L0_ffn_W_k, L0_ffn_W_v,
           L0_ffn_mu_r, L0_ffn_mu_k):
    S = L0_attn_x.shape[1]
    nc = _get_nc(S)
    in_maps = []
    for b in range(NCORES):
        in_maps.append(make_inputs(
            L0_attn_x[b], L0_attn_Wqkv, L0_attn_bqkv, L0_attn_Wo, L0_attn_bo,
            L0_attn_wq, L0_attn_wk, L0_ffn_W_r, L0_ffn_W_k, L0_ffn_W_v,
            L0_ffn_mu_r, L0_ffn_mu_k))
    res = run_bass_kernel_spmd(nc, in_maps, core_ids=list(range(NCORES)))
    return np.stack([res.results[b]["y"] for b in range(NCORES)], axis=0)


# version 06 (final)
# speedup vs baseline: 1.3653x; 1.0120x over previous
import contextlib
import numpy as np
import concourse.bass as bass
import concourse.mybir as mybir
from concourse.bass_utils import run_bass_kernel_spmd

F32 = mybir.dt.float32
BF16 = mybir.dt.bfloat16
F32R = mybir.dt.float32r
AF = mybir.ActivationFunctionType
ALU = mybir.AluOpType
AX = mybir.AxisListType.X

D = 1024
DC = 8
H = 16
DH = 64
NCORES = 8


class Ctx:

    def __init__(self, active, eng, sems):
        self.active = active
        self.eng = eng
        self.h = sems
        self.cnt = {k: 0 for k in sems}

    def _fin(self, instr, inc, n=1):
        if inc is not None:
            self.cnt[inc] += n
            if instr is not None:
                instr.then_inc(self.h[inc], n)
        return instr

    def w(self, eng, sem, val):
        if self.active == eng and val > 0:
            self.eng.wait_ge(self.h[sem], val)

    def now(self, sem):
        return self.cnt[sem]

    def drain(self, eng):
        if self.active == eng and self.cnt[eng] > 0:
            self.eng.wait_ge(self.h[eng], self.cnt[eng])

    def pe(self, f):
        return self._fin(f(self.eng) if self.active == "pe" else None, "pe")

    def dve(self, f):
        return self._fin(f(self.eng) if self.active == "dve" else None, "dve")

    def act(self, f):
        return self._fin(f(self.eng) if self.active == "act" else None, "act")

    def gp(self, f):
        return self._fin(f(self.eng) if self.active == "gp" else None, "gp")

    def dma_sp(self, out, in_, sem):
        i = self.eng.dma_start(out=out, in_=in_) if self.active == "sp" else None
        return self._fin(i, sem, 16)

    def dma_gp(self, out, in_, sem):
        i = self.eng.dma_start(out=out, in_=in_) if self.active == "gp" else None
        return self._fin(i, sem, 16)


def build_nc(S=2048, upto=9):
    KC = S // 128
    QC = S // 512
    TT = S // 128
    nc = bass.Bass("TRN2", target_bir_lowering=False)

    dx = nc.dram_tensor("x", [S, D], F32, kind="ExternalInput")
    dwqkv = nc.dram_tensor("wqkvT", [D, 3 * D], F32, kind="ExternalInput")
    dwo = nc.dram_tensor("woT", [D, D], F32, kind="ExternalInput")
    dwr = nc.dram_tensor("wrT", [D, D], F32, kind="ExternalInput")
    dwk = nc.dram_tensor("wkT", [D, D], F32, kind="ExternalInput")
    dwv = nc.dram_tensor("wvT", [D, D], F32, kind="ExternalInput")
    dbqkv = nc.dram_tensor("bqkv", [128, 24], F32, kind="ExternalInput")
    dbo = nc.dram_tensor("bo", [128, 8], F32, kind="ExternalInput")
    dmur = nc.dram_tensor("mur", [128, 8], F32, kind="ExternalInput")
    dmuk = nc.dram_tensor("muk", [128, 8], F32, kind="ExternalInput")
    dwq2 = nc.dram_tensor("wq2", [128, 1], F32, kind="ExternalInput")
    dwk2 = nc.dram_tensor("wk2", [128, 1], F32, kind="ExternalInput")
    dy = nc.dram_tensor("y", [S, D], F32, kind="ExternalOutput")

    BASE = (nc.sbuf_base + 31) // 32 * 32
    off = [BASE]

    def alloc(name, shape, dtype, at=None):
        per = int(np.prod(shape[1:])) * mybir.dt.size(dtype)
        o = off[0] if at is None else at
        t = nc.alloc_sbuf_tensor_at(name, shape, dtype, offset=o)
        if at is None:
            off[0] = o + (per + 31) // 32 * 32
        return t

    ES = S
    x_sb = alloc("x_sb", [128, DC * S], F32)
    o_qt = off[0]
    qt = alloc("qt", [128, DC * S], BF16)
    xk = alloc("xk", [128, DC * S], BF16, at=o_qt)
    o_kt = off[0]
    kt = alloc("kt", [128, DC * S], BF16)
    rsig = alloc("rsig", [128, DC * S], BF16, at=o_kt)
    o_vt = off[0]
    vt = alloc("vt", [128, KC * H * (DH + 1)], BF16)
    xhat2 = alloc("xhat2", [128, DC * S], BF16, at=o_vt)
    kact = alloc("kact", [128, DC * S], BF16, at=o_vt)
    o_E = off[0]
    E_BYTES = max(2 * DC * S, 4096 + DC * 512 * 2 + 4 * DC * 128 * 2, 2 * D * 4)
    xhat = alloc("xhat", [128, DC * S], BF16, at=o_E)
    off[0] = o_E + E_BYTES
    instage = alloc("instage", [128, 2 * D], F32, at=o_E)
    outstage = alloc("outstage", [128, 2 * D], F32, at=o_E)
    xr = alloc("xr", [128, DC * S], BF16, at=o_E)
    attn = alloc("attn", [128, 2 * 2 * 512], BF16, at=o_E)
    zs = alloc("zs", [128, DC * 512], BF16, at=o_E + 4096)
    wos = alloc("wos", [128, 4 * DC * 128], BF16, at=o_E + 4096 + 8192)
    wst = alloc("wst", [128, 2 * DC * 128], BF16)
    tq = alloc("tq", [128, 512], F32)
    sqb = alloc("sqb", [128, 512], BF16)
    o_xbt = off[0]
    xbt = alloc("xbt", [128, 512], BF16)
    xqt = alloc("xqt", [128, 512], BF16)
    tq2 = alloc("tq2", [128, 512], F32, at=o_xbt)
    stats = alloc("stats", [128, 512], F32)
    statr = alloc("statr", [128, 512], F32R)
    small = alloc("small", [128, 32], F32)
    aones_b = alloc("aones_b", [128, 1], BF16)
    honesA_b = alloc("honesA_b", [128, 1], BF16)
    honesB_b = alloc("honesB_b", [128, 1], BF16)
    bqkv_sb = alloc("bqkv_sb", [128, 24], F32)
    bo_sb = alloc("bo_sb", [128, 8], F32)
    mur_sb = alloc("mur_sb", [128, 8], F32)
    muk_sb = alloc("muk_sb", [128, 8], F32)
    wq2_sb = alloc("wq2_sb", [128, 1], F32)
    wk2_sb = alloc("wk2_sb", [128, 1], F32)
    identf = alloc("identf", [128, 128], F32)
    rowsr = alloc("rowsr", [128, 256], F32R)
    nb = alloc("nb", [128, 8], F32)
    nbr = alloc("nbr", [128, 8], F32R)
    norm_acc = alloc("norm_acc", [128, 72], F32)
    csts = alloc("csts", [128, 8], F32)
    assert off[0] <= 229344, f"sbuf overflow: {off[0]}"
    nc.alloc_sbuf_tensor("reserve_manual", [128, off[0] - BASE], mybir.dt.uint8)
    rowsf = tq
    lrec = stats
    lrec_r = statr
    rz_sb = tq
    sqf = tq

    pA = nc.alloc_psum_tensor("pA", [128, 1024], F32)
    pB = nc.alloc_psum_tensor("pB", [128, 1024], F32)
    pZ = nc.alloc_psum_tensor("pZ", [128, 1024], F32)
    pM = nc.alloc_psum_tensor("pM", [128, 512], F32)
    pT = nc.alloc_psum_tensor("pT", [128, 512], F32)

    nc.const_aps.register(F32, 1e-5, csts[:, 0:1])
    nc.const_aps.register(F32, 1e-6, csts[:, 1:2])
    nc.const_aps.register(F32, 1.0 / DH, csts[:, 2:3])
    nc.const_aps.register(F32, -0.5, csts[:, 5:6])
    nc.const_aps.register(F32, -1.0, csts[:, 6:7])

    sem_names = ["pe", "dve", "act", "gp", "ldp", "lq0", "lq1",
                 "wo0", "wo1", "wo2", "wo3", "ldh0", "ldh1", "ou0", "ou1"]
    cm = contextlib.ExitStack()
    sems = {n: cm.enter_context(nc.semaphore("s_" + n)) for n in sem_names}

    def xs(dc, a, b):
        return x_sb[:, dc * ES + a: dc * ES + b]

    def program(c: Ctx):
        c.gp(lambda e: e.iota(identf[:, :], [[-1, 128]], channel_multiplier=1,
                              allow_small_or_imprecise_dtypes=True))
        c.gp(lambda e: e.memset(rowsf[:, :], 0.0))
        c.drain("gp")
        for _r in (0, 32, 64):
            c.gp(lambda e, _r=_r: e.memset(rowsf[_r:_r + 1, 0:128], 1.0))
            c.drain("gp")
        for _r in (0, 64):
            c.gp(lambda e, _r=_r: e.memset(rowsf[_r:_r + 1, 128:192], 1.0))
            c.drain("gp")
        c.gp(lambda e: e.memset(rowsf[32:33, 192:256], 1.0))
        c.drain("gp")
        c.gp(lambda e: e.memset(aones_b[:, :], 1.0))
        c.drain("gp")
        c.gp(lambda e: e.memset(honesA_b[:, :], 0.0))
        c.drain("gp")
        c.gp(lambda e: e.memset(honesA_b[0:64, :], 1.0))
        c.drain("gp")
        c.gp(lambda e: e.memset(honesB_b[:, :], 0.0))
        c.drain("gp")
        c.gp(lambda e: e.memset(honesB_b[64:128, :], 1.0))
        c.drain("gp")
        c.gp(lambda e: e.memset(vt[:, :], 1.0))
        c.drain("gp")
        c.gp(lambda e: e.memset(csts[:, 0:1], 1e-5))
        c.drain("gp")
        c.gp(lambda e: e.memset(csts[:, 1:2], 1e-6))
        c.drain("gp")
        c.gp(lambda e: e.memset(csts[:, 2:3], 1.0 / DH))
        c.drain("gp")
        c.gp(lambda e: e.memset(csts[:, 3:4], 0.0))
        c.drain("gp")
        c.gp(lambda e: e.memset(csts[:, 4:5], 1.0))
        c.drain("gp")
        c.gp(lambda e: e.memset(csts[:, 5:6], -0.5))
        c.drain("gp")
        c.gp(lambda e: e.memset(csts[:, 6:7], -1.0))
        c.drain("gp")
        gp_const = c.now("gp")

        c.dma_gp(bqkv_sb[:, :], dbqkv.ap(), "ldp")
        c.dma_gp(bo_sb[:, :], dbo.ap(), "ldp")
        c.dma_gp(mur_sb[:, :], dmur.ap(), "ldp")
        c.dma_gp(muk_sb[:, :], dmuk.ap(), "ldp")
        c.dma_gp(wq2_sb[:, :], dwq2.ap(), "ldp")
        c.dma_gp(wk2_sb[:, :], dwk2.ap(), "ldp")
        lds_params = c.now("ldp")

        c.w("dve", "gp", gp_const)
        c.dve(lambda e: e.tensor_single_scalar(identf[:, :], identf[:, :], 0.0, ALU.is_equal))
        c.dve(lambda e: e.tensor_copy(rowsr[:, :], rowsf[:, 0:256]))
        c.w("dve", "ldp", lds_params)
        c.dve(lambda e: e.tensor_mul(small[:, 0:1], wq2_sb[:, :], wk2_sb[:, :]))
        dve_const = c.now("dve")

        dve_tile_mark = []
        for t in range(TT):
            if t >= 2:
                c.w("sp", "dve", dve_tile_mark[t - 2])
            hsem = "ldh" + str(t % 2)
            c.dma_sp(instage[:, (t % 2) * D:(t % 2) * D + D],
                     dx.ap()[t * 128:(t + 1) * 128, :], hsem)
            c.w("pe", hsem, c.now(hsem))
            if t == 0:
                c.w("pe", "dve", dve_const)
            for g in range(2):
                if t > 0 or g > 0:
                    c.w("pe", "dve", c.now("dve"))
                for j in range(4):
                    dcj = g * 4 + j
                    c.pe(lambda e, t=t, dcj=dcj, j=j: e.transpose(
                        pT[:, j * 128: j * 128 + 128],
                        instage[:, (t % 2) * D + dcj * 128:(t % 2) * D + dcj * 128 + 128],
                        identf[:, :]))
                pe_here = c.now("pe")
                c.w("dve", "pe", pe_here)
                for j in range(4):
                    dcj = g * 4 + j
                    c.dve(lambda e, t=t, dcj=dcj, j=j: e.tensor_copy(
                        xs(dcj, t * 128, t * 128 + 128),
                        pT[:, j * 128: j * 128 + 128]))
            dve_tile_mark.append(c.now("dve"))
        x_ready = c.now("dve")

        if upto < 1:
            return finish(c)
        def layernorm(dst, qc):
            q0 = qc * 512
            for dc in range(DC):
                c.w("dve", "pe", c.now("pe"))
                c.dve(lambda e, dc=dc, q0=q0: e.tensor_copy(
                    xbt[:, :], xs(dc, q0, q0 + 512)))
                c.dve(lambda e, dc=dc, q0=q0: e.tensor_mul(
                    xqt[:, :], xs(dc, q0, q0 + 512), xs(dc, q0, q0 + 512)))
                d_here = c.now("dve")
                c.w("pe", "dve", d_here)
                c.pe(lambda e, dc=dc: e.matmul(
                    pB[0:1, 0:512], aones_b[:, :], xbt[:, :],
                    start=(dc == 0), stop=(dc == DC - 1), tile_position=(0, 0)))
                c.pe(lambda e, dc=dc: e.matmul(
                    pB[32:33, 512:1024], aones_b[:, :], xqt[:, :],
                    start=(dc == 0), stop=(dc == DC - 1), tile_position=(0, 32)))
            pe_stats = c.now("pe")
            c.w("dve", "pe", pe_stats)
            c.dve(lambda e: e.tensor_copy(statr[0:1, :], pB[0:1, 0:512]))
            c.dve(lambda e: e.tensor_copy(statr[32:33, :], pB[32:33, 512:1024]))
            dve_sr = c.now("dve")
            c.w("pe", "dve", dve_sr)
            c.pe(lambda e: e.matmul(pM[:, 0:512], rowsr[0:1, 0:128], statr[0:1, :],
                                    start=True, stop=True))
            c.pe(lambda e: e.matmul(pB[:, 0:512], rowsr[32:33, 0:128], statr[32:33, :],
                                    start=True, stop=True))
            pe_bc = c.now("pe")
            c.w("dve", "pe", pe_bc)
            c.drain("dve")
            c.dve(lambda e: e.tensor_scalar(tq[:, 0:512], pM[:, 0:512],
                                            1.0 / D, None, ALU.mult))
            c.drain("dve")
            c.dve(lambda e: e.tensor_mul(stats[:, :], tq[:, 0:512], tq[:, 0:512]))
            c.drain("dve")
            c.dve(lambda e: e.scalar_tensor_tensor(stats[:, :], pB[:, 0:512],
                                                   1.0 / D, stats[:, :],
                                                   ALU.mult, ALU.subtract))
            dve_v = c.now("dve")
            c.w("act", "dve", dve_v)
            c.act(lambda e: e.activation(pM[:, 0:512], stats[:, :], AF.Sqrt,
                                         bias=1e-5, scale=1.0))
            act_sd = c.now("act")
            c.w("dve", "act", act_sd)
            c.dve(lambda e: e.reciprocal(stats[:, :], pM[:, 0:512]))
            c.drain("dve")
            c.dve(lambda e: e.scalar_tensor_tensor(pM[:, 0:512], tq[:, 0:512], -1.0,
                                                   stats[:, :], ALU.mult, ALU.mult))
            c.drain("dve")
            dve_ab = c.now("dve")
            for dc in range(DC):
                c.dve(lambda e, dc=dc, q0=q0: e.tensor_mul(
                    tq[:, 0:512], xs(dc, q0, q0 + 512), stats[:, :]))
                c.drain("dve")
                c.dve(lambda e, dc=dc, q0=q0: e.tensor_add(
                    dst[:, dc * ES + q0: dc * ES + q0 + 512], tq[:, 0:512],
                    pM[:, 0:512]))
                c.drain("dve")
            return c.now("pe"), c.now("dve")

        c.w("pe", "dve", x_ready)
        for qc in range(QC):
            layernorm(xhat, qc)
        ln1_pe, ln1_dve = c.now("pe"), c.now("dve")

        if upto < 2:
            return finish(c)
        def wqkv_ap(oc):
            return bass.AP(dwqkv, oc * 128, [[3 * D, 128], [128 * 3 * D, DC], [1, 128]])

        qkv_pe_mark = [0] * 24
        lds_mark = [0] * 24
        NCH = 24 * QC
        mm_mark = [0] * NCH
        evac_mark = [(0, 0, 0)] * NCH
        eva_act = [0] * NCH

        def emit_chain_mms(i):
            oc, qc = divmod(i, QC)
            if qc == 0:
                if oc == 0:
                    c.dma_gp(wst[:, 0:DC * 128], wqkv_ap(0), "lq0")
                    lds_mark[0] = c.now("lq0")
                    c.dma_gp(wst[:, DC * 128:2 * DC * 128], wqkv_ap(1), "lq1")
                    lds_mark[1] = c.now("lq1")
                elif oc + 1 < 24:
                    c.w("gp", "pe", qkv_pe_mark[oc - 1])
                    sl = ((oc + 1) % 2) * DC * 128
                    c.dma_gp(wst[:, sl:sl + DC * 128], wqkv_ap(oc + 1), "lq" + str((oc + 1) % 2))
                    lds_mark[oc + 1] = c.now("lq" + str((oc + 1) % 2))
                c.w("pe", "lq" + str(oc % 2), lds_mark[oc])
            if i == 0:
                c.w("pe", "dve", ln1_dve)
            bank = (i % 2) * 512
            if i >= 2:
                c.w("pe", "act", eva_act[i - 2])
            wsl = (oc % 2) * DC * 128
            q0 = qc * 512
            for dc in range(DC):
                c.pe(lambda e, dc=dc, q0=q0, bank=bank, wsl=wsl: e.matmul(
                    pA[:, bank:bank + 512],
                    wst[:, wsl + dc * 128: wsl + dc * 128 + 128],
                    xhat[:, dc * ES + q0: dc * ES + q0 + 512],
                    start=(dc == 0), stop=(dc == DC - 1)))
            mm_mark[i] = c.now("pe")
            if qc == QC - 1:
                qkv_pe_mark[oc] = c.now("pe")

        last_stats_pe = [0]
        last_bcast_pe = [0]

        def emit_tail(i):
            oc, qc = divmod(i, QC)
            q0 = qc * 512
            bank = (i % 2) * 512
            pair = oc % 8
            TQ = tq if i % 2 == 0 else tq2
            PBK = pB if i % 2 == 0 else pZ
            PSD = pM if i % 2 == 0 else pT
            c.w("act", "pe", mm_mark[i])
            if i >= 2:
                c.w("act", "dve", evac_mark[i - 2][1])
                c.w("act", "pe", evac_mark[i - 2][2])
            c.act(lambda e, oc=oc, bank=bank, TQ=TQ: e.activation(
                TQ[:, 0:512], pA[:, bank:bank + 512], AF.Identity,
                bias=bqkv_sb[:, oc:oc + 1], scale=1.0))
            eva_act[i] = c.now("act")
            if oc < 16:
                c.w("act", "pe", last_stats_pe[0])
                c.drain("act")
                c.act(lambda e, TQ=TQ: e.activation(sqb[:, :], TQ[:, 0:512], AF.Square))
                act_sq = c.now("act")
                c.w("pe", "act", act_sq)
                if i >= 2:
                    c.w("pe", "dve", evac_mark[i - 2][1])
                c.pe(lambda e, PBK=PBK: e.matmul(PBK[0:1, 0:512], honesA_b[:, :], sqb[:, :],
                                                 start=True, stop=True, tile_position=(0, 0)))
                c.pe(lambda e, PBK=PBK: e.matmul(PBK[32:33, 512:1024], honesB_b[:, :], sqb[:, :],
                                                 start=True, stop=True, tile_position=(0, 32)))
                pe_rms = c.now("pe")
                c.w("dve", "pe", max(pe_rms, last_bcast_pe[0]))
                c.drain("dve")
                c.dve(lambda e, PBK=PBK: e.tensor_copy(statr[0:1, :], PBK[0:1, 0:512]))
                c.dve(lambda e, PBK=PBK: e.tensor_copy(statr[32:33, :], PBK[32:33, 512:1024]))
                dve_sr2 = c.now("dve")
                c.w("pe", "dve", dve_sr2)
                c.pe(lambda e, PBK=PBK: e.matmul(PBK[:, 0:512], rowsr[0:1, 0:128], statr[0:1, :],
                                                 start=True, stop=True))
                c.pe(lambda e, PBK=PBK: e.matmul(PBK[:, 512:1024], rowsr[32:33, 0:128], statr[32:33, :],
                                                 start=True, stop=True))
                pe_ssq = c.now("pe")
                last_bcast_pe[0] = pe_ssq
                c.w("act", "pe", pe_ssq)
                if i >= 2:
                    c.w("act", "dve", evac_mark[i - 2][1])
                c.act(lambda e, PBK=PBK, PSD=PSD: e.activation(
                    PSD[0:64, 0:512], PBK[0:64, 0:512], AF.Sqrt,
                    bias=1e-6, scale=1.0 / DH))
                c.act(lambda e, PBK=PBK, PSD=PSD: e.activation(
                    PSD[64:128, 0:512], PBK[64:128, 512:1024], AF.Sqrt,
                    bias=csts[64:128, 1:2], scale=csts[64:128, 2:3]))
                act_sd2 = c.now("act")
                c.w("dve", "act", act_sd2)
                c.drain("dve")
                c.dve(lambda e, PBK=PBK, PSD=PSD: e.reciprocal(PBK[:, 0:512], PSD[:, 0:512]))
                c.drain("dve")
                if oc < 8:
                    c.dve(lambda e, pair=pair, q0=q0, TQ=TQ, PBK=PBK: e.scalar_tensor_tensor(
                        qt[:, pair * ES + q0: pair * ES + q0 + 512],
                        TQ[:, 0:512], small[:, 0:1], PBK[:, 0:512],
                        ALU.mult, ALU.mult))
                else:
                    c.dve(lambda e, pair=pair, q0=q0, TQ=TQ, PBK=PBK: e.tensor_mul(
                        kt[:, pair * ES + q0: pair * ES + q0 + 512],
                        TQ[:, 0:512], PBK[:, 0:512]))
                dve_apply = c.now("dve")
                dstt = qt if oc < 8 else kt
                c.w("act", "dve", dve_apply)
                c.w("act", "pe", pe_ssq)
                c.drain("act")
                c.act(lambda e, pair=pair, q0=q0, dstt=dstt: e.activation(
                    sqb[:, :], dstt[:, pair * ES + q0: pair * ES + q0 + 512], AF.Square))
                act_csq = c.now("act")
                c.w("pe", "act", act_csq)
                c.pe(lambda e, PSD=PSD: e.matmul(PSD[0:1, 0:512], honesA_b[:, :], sqb[:, :],
                                                 start=True, stop=True, tile_position=(0, 0)))
                c.pe(lambda e, PSD=PSD: e.matmul(PSD[32:33, 0:512], honesB_b[:, :], sqb[:, :],
                                                 start=True, stop=True, tile_position=(0, 32)))
                pe_cs = c.now("pe")
                last_stats_pe[0] = pe_cs
                col = (0 if oc < 8 else 36) + pair * QC + qc
                c.w("dve", "pe", pe_cs)
                c.dve(lambda e, col=col, PSD=PSD: e.reduce_max(
                    norm_acc[0:1, col:col + 1], PSD[0:1, 0:512], axis=AX))
                c.dve(lambda e, col=col, PSD=PSD: e.reduce_max(
                    norm_acc[32:33, col:col + 1], PSD[32:33, 0:512], axis=AX))
            else:
                vc = oc - 16
                c.w("pe", "act", eva_act[i])
                if i >= 2:
                    c.w("pe", "dve", evac_mark[i - 2][1])
                for tb in range(4):
                    c.pe(lambda e, tb=tb, TQ=TQ, PSD=PSD: e.transpose(
                        PSD[:, tb * 128:tb * 128 + 128],
                        TQ[:, tb * 128: tb * 128 + 128], identf[:, :]))
                pe_vt = c.now("pe")
                c.w("dve", "pe", pe_vt)
                for tb in range(4):
                    kc = qc * 4 + tb
                    base = kc * H * (DH + 1) + vc * 2 * (DH + 1)
                    c.dve(lambda e, tb=tb, base=base, PSD=PSD: e.tensor_copy(
                        bass.AP(vt, base, [[KC * H * (DH + 1), 128], [DH + 1, 2], [1, DH]]),
                        PSD[:, tb * 128:tb * 128 + 128].rearrange("p (g n) -> p g n", g=2)))
            evac_mark[i] = (c.now("act"), c.now("dve"), c.now("pe"))

        for i in range(NCH + 1):
            if i < NCH:
                emit_chain_mms(i)
            if i >= 1:
                emit_tail(i - 1)
        qkv_pe = c.now("pe")

        if upto < 3:
            return finish(c)
        c.w("dve", "pe", qkv_pe)
        c.drain("dve")
        for pair in range(8):
            qa, ka = pair * QC, 36 + pair * QC
            for r in (0, 32):
                c.dve(lambda e, qa=qa, r=r: e.reduce_max(
                    small[r:r + 1, 2:3], norm_acc[r:r + 1, qa:qa + QC], axis=AX))
                c.dve(lambda e, ka=ka, r=r: e.reduce_max(
                    small[r:r + 1, 3:4], norm_acc[r:r + 1, ka:ka + QC], axis=AX))
                c.drain("dve")
                c.dve(lambda e, r=r: e.tensor_mul(small[r:r + 1, 4:5],
                                                  small[r:r + 1, 2:3], small[r:r + 1, 3:4]))
                c.drain("dve")
            dve_cs2 = c.now("dve")
            c.w("act", "dve", dve_cs2)
            c.act(lambda e: e.activation(small[0:1, 5:6], small[0:1, 4:5], AF.Sqrt))
            c.act(lambda e: e.activation(small[32:33, 5:6], small[32:33, 4:5], AF.Sqrt,
                                         bias=csts[32:33, 3:4], scale=csts[32:33, 4:5]))
            act_cs = c.now("act")
            c.w("dve", "act", act_cs)
            for r in (0, 32):
                c.dve(lambda e, pair=pair, r=r: e.tensor_scalar(
                    nb[r:r + 1, pair:pair + 1], small[r:r + 1, 5:6],
                    -1.0, 71.0, ALU.mult, ALU.add))
                c.drain("dve")
                c.dve(lambda e, pair=pair, r=r: e.tensor_copy(
                    nbr[r:r + 1, pair:pair + 1], nb[r:r + 1, pair:pair + 1]))
                c.drain("dve")
        dve_nb = c.now("dve")
        c.w("pe", "dve", dve_nb)
        c.pe(lambda e: e.matmul(pM[:, 0:8], rowsr[0:1, 0:128],
                                nbr[0:1, 0:8], start=True, stop=True))
        c.pe(lambda e: e.matmul(pM[:, 8:16], rowsr[32:33, 0:128],
                                nbr[32:33, 0:8], start=True, stop=True))
        pe_nb = c.now("pe")
        c.w("dve", "pe", pe_nb)
        for pair in range(8):
            for hh in range(2):
                c.dve(lambda e, pair=pair, hh=hh: e.tensor_copy(
                    small[:, 16 + 2 * pair + hh:17 + 2 * pair + hh],
                    pM[:, hh * 8 + pair:hh * 8 + pair + 1]))
        dve_bias = c.now("dve")

        if upto < 4:
            return finish(c)
        last_exp = {0: 0, 1: 0}
        last_z = {0: 0, 1: 0}
        wo_dve_mark = 0
        wo_pe_mark = [0] * 4
        lds_wo = [0] * (QC * 8)
        first_att = True

        zevac_dve = 0
        pending_evac = [None]
        zevac_state = {"dve": 0}

        def emit_zevac():
            if pending_evac[0] is None:
                return
            pzpair, z_pe = pending_evac[0]
            pending_evac[0] = None
            for hh in range(2):
                c.w("dve", "pe", z_pe)
                c.dve(lambda e, hh=hh: e.reciprocal(lrec[0:1, 0:512],
                                                    pZ[64:65, hh * 512: hh * 512 + 512]))
                c.drain("dve")
                c.dve(lambda e: e.tensor_copy(statr[0:1, 0:512], lrec[0:1, 0:512]))
                dve_lr = c.now("dve")
                c.w("pe", "dve", dve_lr)
                c.pe(lambda e: e.matmul(pM[0:64, 0:512], rowsr[0:1, 128:192],
                                        statr[0:1, 0:512], start=True, stop=True))
                pe_rz = c.now("pe")
                c.w("dve", "pe", pe_rz)
                c.dve(lambda e: e.tensor_copy(rz_sb[0:64, 0:512], pM[0:64, 0:512]))
                c.drain("dve")
                r0 = hh * 64
                c.dve(lambda e, hh=hh, pzpair=pzpair, r0=r0: e.tensor_mul(
                    zs[r0:r0 + 64, pzpair * 512:(pzpair + 1) * 512],
                    pZ[0:64, hh * 512: hh * 512 + 512], rz_sb[0:64, 0:512]))
            zevac_state["dve"] = c.now("dve")

        for qc in range(QC):
            q0 = qc * 512
            for pair in range(8):
                if first_att:
                    c.w("pe", "dve", dve_bias)
                    first_att = False
                if pair == 0 and qc > 0:
                    c.w("pe", "dve", wo_dve_mark)
                if pair < 4:
                    toc = pair
                    slot = toc % 4
                    if qc == 0:
                        c.w("gp", "pe", qkv_pe)
                    c.w("gp", "pe", wo_pe_mark[slot])
                    c.dma_gp(wos[:, slot * DC * 128:(slot + 1) * DC * 128],
                             bass.AP(dwo, toc * 128, [[D, 128], [128 * D, DC], [1, 128]]),
                             "wo" + str(slot))
                    lds_wo[qc * 8 + toc] = c.now("wo" + str(slot))

                zm = {}
                exp_mark = {}
                zslot_pe = dict(last_z)
                for kc in range(KC + 1):
                    if kc < KC:
                        sl = kc % 2
                        c.w("pe", "act", last_exp[sl])
                        c.pe(lambda e, pair=pair, kc=kc, sl=sl, q0=q0: e.matmul(
                            pA[:, sl * 512: sl * 512 + 512],
                            kt[0:64, pair * ES + kc * 128: pair * ES + kc * 128 + 128],
                            qt[0:64, pair * ES + q0: pair * ES + q0 + 512],
                            start=True, stop=True, tile_position=(0, 0)))
                        c.pe(lambda e, pair=pair, kc=kc, sl=sl, q0=q0: e.matmul(
                            pB[:, sl * 512: sl * 512 + 512],
                            kt[64:128, pair * ES + kc * 128: pair * ES + kc * 128 + 128],
                            qt[64:128, pair * ES + q0: pair * ES + q0 + 512],
                            start=True, stop=True, tile_position=(64, 0)))
                        zm[kc] = c.now("pe")
                    if kc == 1:
                        emit_zevac()
                    if kc >= 1:
                        kz = kc - 1
                        slz = kz % 2
                        c.w("pe", "act", exp_mark[kz])
                        if kz == 0:
                            c.w("pe", "dve", zevac_state["dve"])
                        for hh in range(2):
                            hl = 2 * pair + hh
                            c.pe(lambda e, hh=hh, hl=hl, kz=kz, slz=slz: e.matmul(
                                pZ[0:65, hh * 512: hh * 512 + 512],
                                vt[:, kz * H * (DH + 1) + hl * (DH + 1):
                                   kz * H * (DH + 1) + hl * (DH + 1) + DH + 1],
                                attn[:, (hh * 2 + slz) * 512:(hh * 2 + slz) * 512 + 512],
                                start=(kz == 0), stop=(kz == KC - 1)))
                        last_z[slz] = c.now("pe")
                    if kc < KC:
                        sl = kc % 2
                        c.w("act", "pe", max(zm[kc], last_z[sl] if kc >= 2 else zslot_pe[sl]))
                        c.act(lambda e, pair=pair, sl=sl: e.activation(
                            attn[:, (0 * 2 + sl) * 512:(0 * 2 + sl) * 512 + 512],
                            pA[:, sl * 512: sl * 512 + 512], AF.Exp,
                            bias=small[:, 16 + 2 * pair:17 + 2 * pair], scale=1.0))
                        c.act(lambda e, pair=pair, sl=sl: e.activation(
                            attn[:, (1 * 2 + sl) * 512:(1 * 2 + sl) * 512 + 512],
                            pB[:, sl * 512: sl * 512 + 512], AF.Exp,
                            bias=small[:, 17 + 2 * pair:18 + 2 * pair], scale=1.0))
                        last_exp[sl] = c.now("act")
                        exp_mark[kc] = last_exp[sl]
                z_pe = c.now("pe")
                pending_evac[0] = (pair, z_pe)
            emit_zevac()
            zs_dve = c.now("dve")
            att_act = c.now("act")
            wo_evac = 0
            for oc in range(8):
                bank = (oc % 2) * 512
                c.w("pe", "wo" + str(oc % 4), lds_wo[qc * 8 + oc])
                if oc == 0:
                    c.w("pe", "dve", zs_dve)
                    c.w("pe", "act", att_act)
                if oc >= 2:
                    c.w("pe", "dve", c.now("dve"))
                slot = oc % 4
                for dc in range(DC):
                    c.pe(lambda e, oc=oc, dc=dc, bank=bank, slot=slot: e.matmul(
                        pA[:, bank:bank + 512],
                        wos[:, slot * DC * 128 + dc * 128: slot * DC * 128 + dc * 128 + 128],
                        zs[:, dc * 512:(dc + 1) * 512],
                        start=(dc == 0), stop=(dc == DC - 1)))
                wo_pe_mark[slot] = c.now("pe")
                pe_wo = c.now("pe")
                if oc + 4 < 8:
                    toc = oc + 4
                    c.w("gp", "pe", wo_pe_mark[slot])
                    c.dma_gp(wos[:, slot * DC * 128:(slot + 1) * DC * 128],
                             bass.AP(dwo, toc * 128, [[D, 128], [128 * D, DC], [1, 128]]),
                             "wo" + str(slot))
                    lds_wo[qc * 8 + toc] = c.now("wo" + str(slot))
                c.w("dve", "pe", pe_wo)
                c.dve(lambda e, oc=oc, q0=q0, bank=bank: e.scalar_tensor_tensor(
                    xs(oc, q0, q0 + 512), pA[:, bank:bank + 512], bo_sb[:, oc:oc + 1],
                    xs(oc, q0, q0 + 512), ALU.add, ALU.add))
            wo_dve_mark = c.now("dve")
        att_done_pe = c.now("pe")
        att_done_dve = c.now("dve")

        if upto < 5:
            return finish(c)
        c.w("pe", "dve", att_done_dve)
        for qc in range(QC):
            layernorm(xhat2, qc)
        ln2_dve = c.now("dve")

        for dc in range(DC):
            for qc in range(QC):
                q0 = qc * 512
                a = dc * ES + q0
                w0 = 1 if qc == 0 else 0
                c.drain("dve")
                c.dve(lambda e, a=a, w0=w0: e.tensor_sub(
                    pM[:, w0:512], xhat2[:, a + w0: a + 512], xhat2[:, a + w0 - 1: a + 511]))
                c.drain("dve")
                c.dve(lambda e, a=a, w0=w0, dc=dc: e.scalar_tensor_tensor(
                    xr[:, a + w0: a + 512], pM[:, w0:512], mur_sb[:, dc:dc + 1],
                    xhat2[:, a + w0 - 1: a + 511], ALU.mult, ALU.add))
                c.dve(lambda e, a=a, w0=w0, dc=dc: e.scalar_tensor_tensor(
                    xk[:, a + w0: a + 512], pM[:, w0:512], muk_sb[:, dc:dc + 1],
                    xhat2[:, a + w0 - 1: a + 511], ALU.mult, ALU.add))
                if qc == 0:
                    c.dve(lambda e, a=a, dc=dc: e.tensor_scalar_mul(
                        xr[:, a:a + 1], xhat2[:, a:a + 1], mur_sb[:, dc:dc + 1]))
                    c.dve(lambda e, a=a, dc=dc: e.tensor_scalar_mul(
                        xk[:, a:a + 1], xhat2[:, a:a + 1], muk_sb[:, dc:dc + 1]))
        blend_dve = c.now("dve")

        if upto < 6:
            return finish(c)
        def wffn_ap(dw, oc):
            return bass.AP(dw, oc * 128, [[D, 128], [128 * D, DC], [1, 128]])

        ffn_pe_mark = [0, 0]
        ffn_lds = [0] * 24
        fi = 0
        for phase, (dw, src, kind) in enumerate([
                (dwr, xr, "r"), (dwk, xk, "k"), (dwv, kact, "v")]):
            for oc in range(8):
                gi = phase * 8 + oc
                if gi == 0:
                    c.w("gp", "pe", qkv_pe_mark[23])
                    c.dma_gp(wst[:, 0:DC * 128], wffn_ap(dwr, 0), "lq0")
                    ffn_lds[0] = c.now("lq0")
                    c.dma_gp(wst[:, DC * 128:], wffn_ap(dwr, 1), "lq1")
                    ffn_lds[1] = c.now("lq1")
                elif gi + 1 < 24:
                    nphase, noc = divmod(gi + 1, 8)
                    ndw = [dwr, dwk, dwv][nphase]
                    c.w("gp", "pe", ffn_pe_mark[(gi - 1) % 2])
                    sl = ((gi + 1) % 2) * DC * 128
                    c.dma_gp(wst[:, sl:sl + DC * 128], wffn_ap(ndw, noc), "lq" + str((gi + 1) % 2))
                    ffn_lds[gi + 1] = c.now("lq" + str((gi + 1) % 2))
                c.w("pe", "lq" + str(gi % 2), ffn_lds[gi])
                if gi == 0:
                    c.w("pe", "dve", blend_dve)
                wsl = (gi % 2) * DC * 128
                for qc in range(QC):
                    q0 = qc * 512
                    bank = (fi % 2) * 512
                    if fi >= 2:
                        am, dm, _pm = evac_mark[fi - 2]
                        c.w("pe", "act", am)
                        c.w("pe", "dve", dm)
                    if kind == "v" and fi == 2 * 8 * QC:
                        c.w("pe", "dve", evac_mark[2 * 8 * QC - 1][1])
                    for dc in range(DC):
                        c.pe(lambda e, dc=dc, q0=q0, bank=bank, wsl=wsl, src=src: e.matmul(
                            pA[:, bank:bank + 512],
                            wst[:, wsl + dc * 128: wsl + dc * 128 + 128],
                            src[:, dc * ES + q0: dc * ES + q0 + 512],
                            start=(dc == 0), stop=(dc == DC - 1)))
                    pe_mm = c.now("pe")
                    if kind == "r":
                        c.w("act", "pe", pe_mm)
                        c.act(lambda e, oc=oc, q0=q0, bank=bank: e.activation(
                            rsig[:, oc * ES + q0: oc * ES + q0 + 512],
                            pA[:, bank:bank + 512], AF.Sigmoid))
                    elif kind == "k":
                        c.w("dve", "pe", pe_mm)
                        c.drain("dve")
                        c.dve(lambda e, bank=bank: e.tensor_scalar_max(
                            sqf[:, 0:512], pA[:, bank:bank + 512], 0.0))
                        c.drain("dve")
                        c.dve(lambda e, oc=oc, q0=q0: e.tensor_mul(
                            kact[:, oc * ES + q0: oc * ES + q0 + 512], sqf[:, 0:512], sqf[:, 0:512]))
                    else:
                        c.w("dve", "pe", pe_mm)
                        c.drain("dve")
                        c.dve(lambda e, oc=oc, q0=q0, bank=bank: e.tensor_mul(
                            sqf[:, 0:512], pA[:, bank:bank + 512],
                            rsig[:, oc * ES + q0: oc * ES + q0 + 512]))
                        c.drain("dve")
                        c.dve(lambda e, oc=oc, q0=q0: e.tensor_add(
                            xs(oc, q0, q0 + 512), xs(oc, q0, q0 + 512), sqf[:, 0:512]))
                    evac_mark[fi] = (c.now("act"), c.now("dve"), c.now("pe"))
                    fi += 1
                ffn_pe_mark[gi % 2] = c.now("pe")
        ffn_dve = c.now("dve")
        ffn_act = c.now("act")

        return finish(c)

    def finish(c):
        c.w("pe", "dve", c.now("dve"))
        c.w("pe", "act", c.now("act"))
        c.w("pe", "gp", c.now("gp"))
        out_dve_mark = []
        for t in range(TT):
            for g in range(2):
                c.w("pe", "dve", c.now("dve"))
                for j in range(4):
                    dcj = g * 4 + j
                    c.pe(lambda e, t=t, dcj=dcj, j=j: e.transpose(
                        pT[:, j * 128: j * 128 + 128],
                        xs(dcj, t * 128, t * 128 + 128), identf[:, :]))
                pe_here = c.now("pe")
                c.w("dve", "pe", pe_here)
                if t >= 2 and g == 0:
                    c.w("dve", "ou" + str(t % 2), 16 * ((t - 2) // 2 + 1))
                for j in range(4):
                    dcj = g * 4 + j
                    c.dve(lambda e, t=t, dcj=dcj, j=j, g=g: e.tensor_copy(
                        outstage[:, (t % 2) * D + (g * 4 + j) * 128:
                                 (t % 2) * D + (g * 4 + j) * 128 + 128],
                        pT[:, j * 128: j * 128 + 128]))
            out_dve_mark.append(c.now("dve"))
            c.w("sp", "dve", out_dve_mark[t])
            c.dma_sp(dy.ap()[t * 128:(t + 1) * 128, :],
                     outstage[:, (t % 2) * D:(t % 2) * D + D], "ou" + str(t % 2))
        c.w("sp", "ou0", c.now("ou0"))
        c.w("sp", "ou1", c.now("ou1"))

    with nc.Block() as block:
        @block.tensor
        def _(eng):
            program(Ctx("pe", eng, sems))

        @block.vector
        def _(eng):
            program(Ctx("dve", eng, sems))

        @block.scalar
        def _(eng):
            program(Ctx("act", eng, sems))

        @block.gpsimd
        def _(eng):
            program(Ctx("gp", eng, sems))

        @block.sync
        def _(eng):
            program(Ctx("sp", eng, sems))

    return nc


_NC_CACHE = {}


def _get_nc(S):
    if S not in _NC_CACHE:
        _NC_CACHE[S] = build_nc(S)
    return _NC_CACHE[S]


def make_inputs(b_x, Wqkv, bqkv, Wo, bo, wq, wk, W_r, W_k, W_v, mu_r, mu_k):
    return {
        "x": np.ascontiguousarray(b_x),
        "wqkvT": np.ascontiguousarray(Wqkv.T),
        "woT": np.ascontiguousarray(Wo.T),
        "wrT": np.ascontiguousarray(W_r.T),
        "wkT": np.ascontiguousarray(W_k.T),
        "wvT": np.ascontiguousarray(W_v.T),
        "bqkv": np.ascontiguousarray(bqkv.reshape(24, 128).T),
        "bo": np.ascontiguousarray(bo.reshape(8, 128).T),
        "mur": np.ascontiguousarray(mu_r.reshape(8, 128).T),
        "muk": np.ascontiguousarray(mu_k.reshape(8, 128).T),
        "wq2": np.ascontiguousarray(np.tile(wq, 2).reshape(128, 1)),
        "wk2": np.ascontiguousarray(np.tile(wk, 2).reshape(128, 1)),
    }


def kernel(L0_attn_x, L0_attn_Wqkv, L0_attn_bqkv, L0_attn_Wo, L0_attn_bo,
           L0_attn_wq, L0_attn_wk, L0_ffn_W_r, L0_ffn_W_k, L0_ffn_W_v,
           L0_ffn_mu_r, L0_ffn_mu_k):
    S = L0_attn_x.shape[1]
    nc = _get_nc(S)
    in_maps = []
    for b in range(NCORES):
        in_maps.append(make_inputs(
            L0_attn_x[b], L0_attn_Wqkv, L0_attn_bqkv, L0_attn_Wo, L0_attn_bo,
            L0_attn_wq, L0_attn_wk, L0_ffn_W_r, L0_ffn_W_k, L0_ffn_W_v,
            L0_ffn_mu_r, L0_ffn_mu_k))
    res = run_bass_kernel_spmd(nc, in_maps, core_ids=list(range(NCORES)))
    return np.stack([res.results[b]["y"] for b in range(NCORES)], axis=0)
